# version 10
# baseline (speedup 1.0000x reference)
import numpy as np
from contextlib import ExitStack

import concourse.bass as bass
import concourse.mybir as mybir
from concourse.alu_op_type import AluOpType as OP
from concourse.bass_utils import run_bass_kernel_spmd

F32 = mybir.dt.float32
BF16 = mybir.dt.bfloat16
AF = mybir.ActivationFunctionType
AXX = mybir.AxisListType.X

D = 1024
HID = 4096
TOK = 2048
TT = 512
NT = TOK // TT
DC = D // 128
HC = HID // 128
NQ = TOK // 128
NEG = -1.0e9
EPS = 1e-5
RGROUPS = [[0, 1], [2, 3], [4, 5], [6, 7]]

DEBUG = False
PHASES = 4
DBG_J = 0


class Sem:
    __slots__ = ("h", "n")

    def __init__(self, h):
        self.h = h
        self.n = 0


class PB:
    ENGINES = ("tensor", "vector", "scalar", "sync", "gpsimd")

    def __init__(self, nc, st):
        self.nc = nc
        self.st = st
        self.ops = {e: [] for e in self.ENGINES}

    def sem(self, name):
        return Sem(self.st.enter_context(self.nc.semaphore(name)))

    def emit(self, eng, fn):
        self.ops[eng].append(fn)

    def wait(self, eng, sem, val=None):
        v = sem.n if val is None else val
        if v <= 0:
            return
        self.emit(eng, lambda e, h=sem.h, v=v: e.wait_ge(h, v))

    def wait2(self, eng, sv):
        if sv is None:
            return
        if isinstance(sv, list):
            for x in sv:
                self.wait2(eng, x)
            return
        if sv[0] == "full":
            self.wait(eng, sv[1], sv[1].n)
            return
        self.wait(eng, sv[0], sv[1])

    def ins(self, eng, build, inc=None, by=1):
        if inc is None:
            self.emit(eng, build)
            return None
        inc.n += by
        self.emit(eng, lambda e, b=build, h=inc.h, by=by: b(e).then_inc(h, by))
        return inc.n


class Slot:

    def __init__(self, p, name):
        self.p = p
        self.sem = p.sem(name)
        self.free = None

    def load(self, queue, builds, extra_wait=None):
        self.p.wait2(queue, self.free)
        self.free = None
        if extra_wait is not None:
            self.p.wait2(queue, extra_wait)
        for b in builds:
            self.p.ins(queue, b, self.sem, 16)
        return self.sem.n

    def wait_loaded(self, eng, val=None):
        self.p.wait(eng, self.sem, self.sem.n if val is None else val)

    def set_free(self, sv):
        self.free = sv


def build_program():
    nc = bass.Bass("TRN2", target_bir_lowering=False)
    st = ExitStack()
    p = PB(nc, st)

    x_in = nc.dram_tensor("x", [D, TOK], F32, kind="ExternalInput")
    wdr = {}
    for nm, shp in [("w1", [HID, D]), ("w2", [HID, D]), ("h1", [HID, HID]),
                    ("h2", [HID, HID]), ("w3", [D, HID]),
                    ("wq", [D, D]), ("wk", [D, D]), ("wv", [D, D])]:
        wdr[nm] = nc.dram_tensor(nm, shp, F32, kind="ExternalInput")
    sdr = {}
    for nm, n in [("s1", HID), ("s2", HID), ("s3", HID), ("s4", HID), ("s5", D)]:
        sdr[nm] = nc.dram_tensor(nm, [128, n // 128], F32, kind="ExternalInput")
    m0_in = nc.dram_tensor("mask0", [128, 128], F32, kind="ExternalInput")
    m1_in = nc.dram_tensor("mask1", [128, 128], F32, kind="ExternalInput")
    eyeb_in = nc.dram_tensor("eyeb", [128, 128], BF16, kind="ExternalInput")
    eyef_in = nc.dram_tensor("eyef", [128, 128], F32, kind="ExternalInput")
    ones_in = nc.dram_tensor("ones", [128, 128], F32, kind="ExternalInput")
    y_out = nc.dram_tensor("y", [D, TOK], F32, kind="ExternalOutput")

    wshape = {nm: list(wdr[nm].shape) for nm in ("w1", "w2", "h1", "h2", "w3")}
    wb = {nm: nc.dram_tensor("wb_" + nm, wshape[nm], BF16) for nm in wshape}
    wt = {nm: nc.dram_tensor("wt_" + nm, [wshape[nm][1], wshape[nm][0]], BF16)
          for nm in wshape}
    wspl = {}
    for wnm in ("wq", "wk", "wv"):
        for hl in ("h", "l"):
            wspl[(wnm, hl)] = nc.dram_tensor(f"{wnm}_{hl}", [D, D], BF16)
    r_dram = nc.dram_tensor("r_dram", [D, TOK], F32)
    q_dram = nc.dram_tensor("q_dram", [2, D, TOK], BF16)
    kin = [nc.dram_tensor(f"kin{t}", [2, D, TT], BF16) for t in range(NT)]
    vin = [nc.dram_tensor(f"vin{t}", [2, TT, D], BF16) for t in range(NT)]
    kg = [nc.dram_tensor(f"kg{t}", [4, D, TT], BF16) for t in range(NT)]
    vg = [nc.dram_tensor(f"vg{t}", [4, TT, D], BF16) for t in range(NT)]

    sb = st.enter_context
    ones_f = sb(nc.sbuf_tensor("ones_f", [128, 128], F32))
    eye_b = sb(nc.sbuf_tensor("eye_b", [128, 128], BF16))
    eye_f = sb(nc.sbuf_tensor("eye_f", [128, 128], F32))
    mask0 = sb(nc.sbuf_tensor("mask0s", [128, 128], F32))
    mask1 = sb(nc.sbuf_tensor("mask1s", [128, 128], F32))
    s1s = sb(nc.sbuf_tensor("s1s", [128, HC], F32))
    s2s = sb(nc.sbuf_tensor("s2s", [128, HC], F32))
    s3s = sb(nc.sbuf_tensor("s3s", [128, HC], F32))
    s4s = sb(nc.sbuf_tensor("s4s", [128, HC], F32))
    s5s = sb(nc.sbuf_tensor("s5s", [128, DC], F32))
    rstd = sb(nc.sbuf_tensor("rstd", [128, TT], F32))
    vtmp = sb(nc.sbuf_tensor("vtmp", [128, TT], F32))
    negmx = sb(nc.sbuf_tensor("negmx", [128, 1], F32))
    sume = sb(nc.sbuf_tensor("sume", [128, 8], F32))
    sumt = sb(nc.sbuf_tensor("sumt", [128, 1], F32))
    rs = sb(nc.sbuf_tensor("rs", [128, 1], F32))
    sq = [sb(nc.sbuf_tensor(f"sq{i}", [128, TT], F32)) for i in range(2)]
    stg = [sb(nc.sbuf_tensor(f"stg{i}", [128, TT], F32)) for i in range(2)]
    xn_h = sb(nc.sbuf_tensor("xn_h", [128, DC * TT], BF16))
    xn_l = sb(nc.sbuf_tensor("xn_l", [128, DC * TT], BF16))
    R = sb(nc.sbuf_tensor("R", [128, DC * TT], F32))
    ws = sb(nc.sbuf_tensor("ws", [128, 16384], BF16))

    pb4 = [sb(nc.psum_tensor(f"pb{i}", [128, 512], F32)) for i in range(4)]
    pO = sb(nc.psum_tensor("pO", [128, 1024], F32))
    pt2 = [sb(nc.psum_tensor(f"pt{i}", [128, 1024], BF16)) for i in range(2)]

    s_g = p.sem("s_g")
    s_cc = p.sem("s_cc")
    s_mm = p.sem("s_mm")
    s_vv = p.sem("s_vv")
    s_vs = p.sem("s_vs")
    s_bst = p.sem("s_bst")
    s_w3st = p.sem("s_w3st")
    s_pst = [p.sem("s_pst0"), p.sem("s_pst1")]
    s_vst = [p.sem("s_vst0"), p.sem("s_vst1")]
    s_ost = p.sem("s_ost")

    ms_slot = Slot(p, "ms")
    wsS = [Slot(p, f"ws{i}") for i in range(4)]
    sqS = [Slot(p, "sqs0"), Slot(p, "sqs1")]
    kstS = [Slot(p, "kst0"), Slot(p, "kst1")]
    vstS = [Slot(p, "vst0"), Slot(p, "vst1")]
    qtS = [Slot(p, "qt0"), Slot(p, "qt1")]
    r_slot = Slot(p, "r_sl")
    rq_slot = Slot(p, "rq_sl")
    wslot = Slot(p, "wsl")

    bank_free = {i: None for i in range(4)}
    bank_mmv = {i: 0 for i in range(4)}
    state = {"pO_free": None}
    pt_free = {0: None, 1: None}
    sqF = [None, None]
    stF = [None, None]
    oc_rot = [0]

    ms_builds = [lambda e: e.dma_start(out=mask0[:, :], in_=m0_in[:, :]),
                 lambda e: e.dma_start(out=mask1[:, :], in_=m1_in[:, :]),
                 lambda e: e.dma_start(out=eye_b[:, :], in_=eyeb_in[:, :]),
                 lambda e: e.dma_start(out=eye_f[:, :], in_=eyef_in[:, :]),
                 lambda e: e.dma_start(out=ones_f[:, :], in_=ones_in[:, :])]
    for s_t, nm in [(s1s, "s1"), (s2s, "s2"), (s3s, "s3"), (s4s, "s4"), (s5s, "s5")]:
        ms_builds.append(lambda e, o=s_t, i=sdr[nm]: e.dma_start(out=o[:, :], in_=i[:, :]))
    ms_slot.load("sync", ms_builds)

    ms_slot.wait_loaded("tensor")
    ms_slot.wait_loaded("scalar")
    ms_slot.wait_loaded("vector")
    p.ins("vector", lambda e: e.tensor_scalar(out=s5s[:, :], in0=s5s[:, :], scalar1=0.5,
                                              scalar2=None, op0=OP.mult), s_vv, 1)
    s5_ready = s_vv.n
    s_wsp = p.sem("s_wsp")

    xnpF = [None, None]

    def vec_convert(jobs):
        for ji, (sap, outs) in enumerate(jobs):
            k = ji % 2
            sqS[k].set_free(sqF[k])
            ldv = sqS[k].load("gpsimd",
                              [lambda e, s=sap, k=k: e.dma_start(out=sq[k][:, :], in_=s)])
            sqS[k].wait_loaded("vector", ldv)
            p.wait2("vector", xnpF[k])
            xh = xn_h[:, k * 512:(k + 1) * 512]
            xl = xn_l[:, k * 512:(k + 1) * 512]
            p.ins("vector", lambda e, k=k, xh=xh: e.tensor_copy(out=xh, in_=sq[k][:, :]),
                  s_vv, 1)
            if any(kind == "l" for _, kind in outs):
                p.wait("vector", s_vv, s_vv.n)
                p.ins("vector", lambda e, k=k, xh=xh, xl=xl:
                      e.tensor_tensor(out=xl, in0=sq[k][:, :], in1=xh, op=OP.subtract),
                      s_vv, 1)
            vv = s_vv.n
            sqF[k] = (s_vv, vv)
            p.wait("gpsimd", s_vv, vv)
            for dap, kind in outs:
                p.ins("gpsimd",
                      lambda e, d=dap, s=(xh if kind == "h" else xl):
                      e.dma_start(out=d, in_=s), s_wsp, 16)
            xnpF[k] = ("full", s_wsp)

    def split_attn_weights():
        p.wait("vector", s_mm, s_mm.n)
        jobs = []
        for wnm in ("wq", "wk", "wv"):
            for ch in range(16):
                dc2, e2 = divmod(ch, 2)
                sap = wdr[wnm][128 * dc2:128 * (dc2 + 1), 512 * e2:512 * (e2 + 1)]
                outs = [(wspl[(wnm, hl)][128 * dc2:128 * (dc2 + 1),
                                         512 * e2:512 * (e2 + 1)], hl)
                        for hl in ("h", "l")]
                jobs.append((sap, outs))
        vec_convert(jobs)


    s_wcv = p.sem("s_wcv")
    wcv_ldv = 0
    for nm in ("w1", "w2"):
        for rb in range(8):
            srcw = wdr[nm][512 * rb:512 * (rb + 1), :].rearrange(
                "(c p) e -> p c e", p=128)
            dstw = wb[nm][512 * rb:512 * (rb + 1), :].rearrange(
                "(c p) e -> p c e", p=128)
            rview = R.ap().rearrange("p (c e) -> p c e", e=D)
            xview = xn_h.ap().rearrange("p (c e) -> p c e", e=D)
            p.wait("sync", s_wcv, s_wcv.n)
            wcv_ldv = p.ins("sync", lambda e, s=srcw, d=rview:
                            e.dma_start(out=d, in_=s), s_wcv, 16)
            p.wait("vector", s_wcv, wcv_ldv)
            cvv = p.ins("vector", lambda e: e.tensor_copy(out=xn_h[:, :], in_=R[:, :]),
                        s_vv, 1)
            p.wait("sync", s_vv, cvv)
            p.ins("sync", lambda e, d=dstw, s=xview: e.dma_start(out=d, in_=s),
                  s_wcv, 16)
    wcv_done = s_wcv.n
    p.wait("vector", s_wcv, wcv_done)

    cast_done = {"w1": None, "w2": None}
    for nm in ("h1", "h2", "w3"):
        rows = wshape[nm][0]
        nch = 4
        step = rows // nch
        for c in range(nch):
            p.ins("gpsimd",
                  lambda e, nm=nm, a=c * step, b=(c + 1) * step:
                  e.dma_start(out=wb[nm][a:b, :], in_=wdr[nm][a:b, :]), s_g, 16)
        cast_done[nm] = s_g.n
        p.wait("gpsimd", s_g, s_g.n)

    wt_ready = {}
    b_chunks = []

    def stage_b(nm):
        rows, cols = wshape[nm]
        n_ic = cols // 128
        for ic in range(n_ic):
            gi = len(b_chunks)
            slot = wsS[2 + gi % 2]
            sc = (2 + gi % 2) * 4096
            if ic < 2:
                gate = (s_wcv, wcv_done) if nm in ("w1", "w2") else (s_g, cast_done[nm])
            else:
                gate = None
            ldv = slot.load("sync",
                            [lambda e, nm=nm, ic=ic, sc=sc, rr=rows:
                             e.dma_start(out=ws[:, sc:sc + rr],
                                         in_=wb[nm][:, 128 * ic:128 * (ic + 1)],
                                         transpose=True)],
                            extra_wait=gate)
            b_chunks.append((nm, ic))
            slot.wait_loaded("sync", ldv)
            p.ins("sync",
                  lambda e, nm=nm, ic=ic, sc=sc, rr=rows:
                  e.dma_start(out=wt[nm][128 * ic:128 * (ic + 1), :],
                              in_=ws[:, sc:sc + rr]), s_bst, 16)
            slot.set_free(("full", s_bst))
        wt_ready[nm] = s_bst.n

    def ln_emit(r_ready):
        p.wait2("tensor", state["pO_free"])
        p.wait2("tensor", r_ready)
        p.wait2("vector", r_ready)
        p.wait("vector", s_vv, s_vv.n)
        for ci in range(DC):
            p.ins("tensor",
                  lambda e, ci=ci: e.matmul(pO[:, 0:512], ones_f[:, :],
                                            R[:, ci * TT:(ci + 1) * TT],
                                            start=(ci == 0), stop=(ci == DC - 1)),
                  s_mm if ci == DC - 1 else None, 1)
        mean_mm = s_mm.n
        p.wait("vector", s_mm, mean_mm)
        for ci in range(DC):
            k = ci % 2
            p.wait2("vector", sqF[k])
            vv = p.ins("vector",
                       lambda e, ci=ci, k=k: e.scalar_tensor_tensor(
                           out=sq[k][:, :], in0=pO[:, 0:512], scalar=-1.0 / D,
                           in1=R[:, ci * TT:(ci + 1) * TT], op0=OP.mult, op1=OP.add),
                       s_vv, 1)
            p.wait("scalar", s_vv, vv)
            p.wait2("scalar", stF[k])
            sv = p.ins("scalar", lambda e, k=k: e.square(stg[k][:, :], sq[k][:, :]), s_vs, 1)
            sqF[k] = (s_vs, sv)
            p.wait("tensor", s_vs, sv)
            mv = p.ins("tensor",
                       lambda e, ci=ci, k=k: e.matmul(pO[:, 512:1024], ones_f[:, :],
                                                      stg[k][:, :],
                                                      start=(ci == 0), stop=(ci == DC - 1)),
                       s_mm, 1)
            stF[k] = (s_mm, mv)
        var_mm = s_mm.n
        p.wait("vector", s_mm, var_mm)
        p.ins("vector", lambda e: e.tensor_scalar(out=vtmp[:, :], in0=pO[:, 512:1024],
                                                  scalar1=1.0 / D, scalar2=EPS,
                                                  op0=OP.mult, op1=OP.add), s_vv, 1)
        p.wait("scalar", s_vv, s_vv.n)
        p.ins("scalar", lambda e: e.activation(out=rstd[:, :], in_=vtmp[:, :], func=AF.Sqrt),
              s_vs, 1)
        p.wait("vector", s_vs, s_vs.n)
        p.ins("vector", lambda e: e.reciprocal(out=rstd[:, :], in_=rstd[:, :]), s_vv, 1)
        pings = [sq[0], sq[1], stg[0], stg[1]]
        pingF = [sqF[0], sqF[1], stF[0], stF[1]]
        for ci in range(DC):
            k = ci % 4
            pg = pings[k]
            p.wait2("vector", pingF[k])
            p.ins("vector",
                  lambda e, ci=ci, pg=pg: e.scalar_tensor_tensor(
                      out=pg[:, :], in0=pO[:, 0:512], scalar=-1.0 / D,
                      in1=R[:, ci * TT:(ci + 1) * TT], op0=OP.mult, op1=OP.add), s_vv, 1)
            p.wait("vector", s_vv, s_vv.n)
            p.ins("vector",
                  lambda e, pg=pg: e.tensor_tensor(out=pg[:, :], in0=pg[:, :],
                                                   in1=rstd[:, :], op=OP.mult), s_vv, 1)
            p.wait("vector", s_vv, s_vv.n)
            p.ins("vector",
                  lambda e, ci=ci, pg=pg: e.tensor_copy(out=xn_h[:, ci * TT:(ci + 1) * TT],
                                                        in_=pg[:, :]), s_vv, 1)
            p.wait("vector", s_vv, s_vv.n)
            vv = p.ins("vector",
                       lambda e, ci=ci, pg=pg: e.tensor_tensor(
                           out=xn_l[:, ci * TT:(ci + 1) * TT], in0=pg[:, :],
                           in1=xn_h[:, ci * TT:(ci + 1) * TT], op=OP.subtract), s_vv, 1)
            pingF[k] = (s_vv, vv)
        sqF[0], sqF[1], stF[0], stF[1] = pingF
        xn_ready = (s_vv, s_vv.n)
        state["pO_free"] = (s_vv, s_vv.n)
        return xn_ready

    def load_slab(nm, sl, n_ic, wo, slot_idx):
        slot = wsS[slot_idx]
        base = slot_idx * 4096
        src = wt[nm][:, sl * wo:(sl + 1) * wo].rearrange("(ci p) o -> p ci o", p=128)
        dst = ws[:, base:base + n_ic * wo].rearrange("p (ci o) -> p ci o", o=wo)
        gw = ("full", s_bst) if load_slab.gate else None
        ldv = slot.load("sync", [lambda e, s=src, d=dst: e.dma_start(out=d, in_=s)],
                        extra_wait=gw)
        cols = {ci: base + ci * wo for ci in range(n_ic)}
        return slot, ldv, cols

    load_slab.gate = False

    def tlin_emit(nm, n_ic, n_o, rhs_h, rhs_l, rhs_ready, evac_fn, first_tile):
        load_slab.gate = first_tile
        wo = 4096 // n_ic
        for sl in range(n_o * 128 // wo):
            if first_tile and sl >= 2:
                load_slab.gate = False
            slot, ldv, cols = load_slab(nm, sl, n_ic, wo, sl % 2)
            for oc2 in range(wo // 128):
                oc = sl * (wo // 128) + oc2
                bank = oc_rot[0] % 4
                oc_rot[0] += 1
                p.wait2("tensor", bank_free[bank])
                if oc2 == 0:
                    slot.wait_loaded("tensor", ldv)
                    if sl == 0:
                        p.wait2("tensor", rhs_ready)
                pbk = pb4[bank]
                for ci in range(n_ic):
                    lh = ws[:, cols[ci] + oc2 * 128: cols[ci] + oc2 * 128 + 128]
                    p.ins("tensor",
                          lambda e, lh=lh, ci=ci, pbk=pbk:
                          e.matmul(pbk[:, :], lh, rhs_h[:, ci * TT:(ci + 1) * TT],
                                   start=(ci == 0), stop=False))
                    p.ins("tensor",
                          lambda e, lh=lh, ci=ci, pbk=pbk, last=(ci == n_ic - 1):
                          e.matmul(pbk[:, :], lh, rhs_l[:, ci * TT:(ci + 1) * TT],
                                   start=False, stop=last),
                          s_mm if ci == n_ic - 1 else None, 1)
                bank_mmv[bank] = s_mm.n
                evac_fn(oc, bank)
            slot.set_free((s_mm, s_mm.n))
        load_slab.gate = False

    ctx = {"r_guard": None, "rdram_ready": None}

    def ffn_tile(t, src_dram, dst_dram, first_tile, copy_back_R, bufs,
                 pre_layer=None):
        ha_h, ha_l, hb_h, hb_l = bufs
        p.wait("vector", s_w3st, s_w3st.n)
        p.wait("vector", s_ost, s_ost.n)
        p.wait("vector", s_pst[0], s_pst[0].n)
        p.wait("vector", s_pst[1], s_pst[1].n)
        p.wait("vector", s_vst[0], s_vst[0].n)
        p.wait("vector", s_vst[1], s_vst[1].n)
        src = src_dram.ap().rearrange("(dc p) tok -> p dc tok", p=128)[:, :, t * TT:(t + 1) * TT]
        dst = R.ap().rearrange("p (dc tok) -> p dc tok", tok=TT)
        extra = ctx["rdram_ready"] if src_dram is r_dram else None
        r_slot.set_free(ctx["r_guard"])
        ldv = r_slot.load("sync", [lambda e, s=src, d=dst: e.dma_start(out=d, in_=s)],
                          extra_wait=extra)
        xn_ready = ln_emit((r_slot.sem, ldv))

        silu_pend = {}
        load_slab.gate = first_tile
        for sl in range(HID // 512):
            if first_tile and sl >= 1:
                load_slab.gate = False
            slot1, ldv1, cols1 = load_slab("w1", sl, DC, 512, 0)
            slot2, ldv2, cols2 = load_slab("w2", sl, DC, 512, 1)
            for half in range(2):
                for nm in ("w1", "w2"):
                    slot, ldv, cols = (slot1, ldv1, cols1) if nm == "w1" else                                       (slot2, ldv2, cols2)
                    for oc2 in (2 * half, 2 * half + 1):
                        oc = sl * 4 + oc2
                        bank = oc_rot[0] % 4
                        oc_rot[0] += 1
                        p.wait2("tensor", bank_free[bank])
                        if oc2 == 2 * half and half == 0:
                            slot.wait_loaded("tensor", ldv)
                            if sl == 0 and nm == "w1":
                                p.wait2("tensor", xn_ready)
                        pbk = pb4[bank]
                        for ci in range(DC):
                            lh = ws[:, cols[ci] + oc2 * 128: cols[ci] + oc2 * 128 + 128]
                            p.ins("tensor",
                                  lambda e, lh=lh, ci=ci, pbk=pbk:
                                  e.matmul(pbk[:, :], lh, xn_h[:, ci * TT:(ci + 1) * TT],
                                           start=(ci == 0), stop=False))
                            p.ins("tensor",
                                  lambda e, lh=lh, ci=ci, pbk=pbk, last=(ci == DC - 1):
                                  e.matmul(pbk[:, :], lh, xn_l[:, ci * TT:(ci + 1) * TT],
                                           start=False, stop=last),
                                  s_mm if ci == DC - 1 else None, 1)
                        bank_mmv[bank] = s_mm.n
                        if nm == "w1":
                            k = oc % 2
                            p.wait("scalar", s_mm, bank_mmv[bank])
                            p.wait2("scalar", stF[k])
                            sv = p.ins("scalar",
                                       lambda e, oc=oc, bank=bank, k=k:
                                       e.activation(out=stg[k][:, :], in_=pb4[bank][:, :],
                                                    func=AF.Silu,
                                                    scale=s1s[:, oc:oc + 1]), s_vs, 1)
                            bank_free[bank] = (s_vs, sv)
                            silu_pend[oc] = (k, sv)
                        else:
                            k, sv = silu_pend.pop(oc)
                            p.wait("vector", s_mm, bank_mmv[bank])
                            p.wait("vector", s_vs, sv)
                            p.wait2("vector", sqF[k])
                            p.ins("vector",
                                  lambda e, oc=oc, bank=bank, k=k:
                                  e.scalar_tensor_tensor(out=sq[k][:, :],
                                                         in0=pb4[bank][:, :],
                                                         scalar=s2s[:, oc:oc + 1],
                                                         in1=stg[k][:, :],
                                                         op0=OP.mult, op1=OP.mult),
                                  s_vv, 1)
                            stF[k] = (s_vv, s_vv.n)
                            p.wait("vector", s_vv, s_vv.n)
                            p.ins("vector",
                                  lambda e, oc=oc, k=k:
                                  e.tensor_copy(out=ha_h[:, oc * TT:(oc + 1) * TT],
                                                in_=sq[k][:, :]), s_vv, 1)
                            p.wait("vector", s_vv, s_vv.n)
                            vv = p.ins("vector",
                                       lambda e, oc=oc, k=k: e.tensor_tensor(
                                           out=ha_l[:, oc * TT:(oc + 1) * TT],
                                           in0=sq[k][:, :],
                                           in1=ha_h[:, oc * TT:(oc + 1) * TT],
                                           op=OP.subtract), s_vv, 1)
                            bank_free[bank] = (s_vv, vv)
                            sqF[k] = (s_vv, vv)
            slot1.set_free((s_mm, s_mm.n))
            slot2.set_free((s_mm, s_mm.n))
        load_slab.gate = False
        ha_ready = (s_vv, s_vv.n)

        def evac_h(dst_h, dst_l, ssc):
            def f(oc, bank):
                k = oc % 2
                p.wait("scalar", s_mm, bank_mmv[bank])
                p.wait2("scalar", stF[k])
                sv = p.ins("scalar",
                           lambda e, oc=oc, bank=bank, k=k:
                           e.activation(out=stg[k][:, :], in_=pb4[bank][:, :], func=AF.Gelu,
                                        scale=ssc[:, oc:oc + 1]), s_vs, 1)
                bank_free[bank] = (s_vs, sv)
                p.wait("vector", s_vs, sv)
                p.ins("vector",
                      lambda e, oc=oc: e.tensor_copy(out=dst_h[:, oc * TT:(oc + 1) * TT],
                                                     in_=stg[k][:, :]), s_vv, 1)
                p.wait("vector", s_vv, s_vv.n)
                vv = p.ins("vector",
                           lambda e, oc=oc: e.tensor_tensor(
                               out=dst_l[:, oc * TT:(oc + 1) * TT], in0=stg[k][:, :],
                               in1=dst_h[:, oc * TT:(oc + 1) * TT], op=OP.subtract),
                           s_vv, 1)
                stF[k] = (s_vv, vv)
            return f

        if pre_layer:
            pre_layer("h1")
        tlin_emit("h1", HC, HC, ha_h, ha_l, ha_ready, evac_h(hb_h, hb_l, s3s), first_tile)
        hb_ready = (s_vv, s_vv.n)
        if pre_layer:
            pre_layer("h2")
        tlin_emit("h2", HC, HC, hb_h, hb_l, hb_ready, evac_h(ha_h, ha_l, s4s), first_tile)
        ha2_ready = (s_vv, s_vv.n)

        w3pings = [sq[0], sq[1], stg[0], stg[1]]
        w3F = [sqF[0], sqF[1], stF[0], stF[1]]

        def evac_w3(oc, bank):
            k = oc % 4
            pg = w3pings[k]
            p.wait("vector", s_mm, bank_mmv[bank])
            if first_tile and oc == 0:
                p.wait("vector", s_vv, s5_ready)
            p.wait2("vector", w3F[k])
            p.wait("vector", s_vv, s_vv.n)
            vv = p.ins("vector",
                       lambda e, oc=oc, bank=bank, pg=pg:
                       e.scalar_tensor_tensor(out=pg[:, :], in0=pb4[bank][:, :],
                                              scalar=s5s[:, oc:oc + 1],
                                              in1=R[:, oc * TT:(oc + 1) * TT],
                                              op0=OP.mult, op1=OP.add), s_vv, 1)
            bank_free[bank] = (s_vv, vv)
            p.wait("scalar", s_vv, vv)
            p.ins("scalar",
                  lambda e, oc=oc, pg=pg, t=t, dd=dst_dram:
                  e.dma_start(out=dd[oc * 128:(oc + 1) * 128, t * TT:(t + 1) * TT],
                              in_=pg[:, :]), s_w3st, 16)
            if copy_back_R:
                p.wait("vector", s_vv, s_vv.n)
                cv = p.ins("vector",
                           lambda e, oc=oc, pg=pg: e.tensor_copy(
                               out=R[:, oc * TT:(oc + 1) * TT], in_=pg[:, :]), s_vv, 1)
                w3F[k] = [(s_vv, cv), ("full", s_w3st)]
            else:
                w3F[k] = ("full", s_w3st)

        if pre_layer:
            pre_layer("w3")
        tlin_emit("w3", HC, DC, ha_h, ha_l, ha2_ready, evac_w3, first_tile)
        sqF[0], sqF[1], stF[0], stF[1] = w3F
        return (s_vv, s_vv.n)

    kq_stage_free = [None, None]
    v_stage_free = [None, None]

    def qkv_tile(t, bufs):
        ha_h, ha_l, hb_h, hb_l = bufs
        xn_ready = ln_emit((s_vv, s_vv.n))
        qkv_tile.r_free = [xn_ready, (s_mm, s_mm.n)]
        WB = 3072
        Wh = hb_h
        Wl = hb_l
        for iw, (wnm, kind) in enumerate((("wq", "q"), ("wk", "k"), ("wv", "v"))):
            srch = wspl[(wnm, "h")].ap().rearrange("(dc p) e -> p dc e", p=128)
            srcl = wspl[(wnm, "l")].ap().rearrange("(dc p) e -> p dc e", p=128)
            dsth = hb_h[:, WB:WB + 8192].rearrange("p (dc e) -> p dc e", e=1024)
            dstl = hb_l[:, WB:WB + 8192].rearrange("p (dc e) -> p dc e", e=1024)
            gw = [("full", s_wsp), (s_mm, s_mm.n)]
            ldv = wslot.load("sync",
                             [lambda e, s=srch, d=dsth: e.dma_start(out=d, in_=s),
                              lambda e, s=srcl, d=dstl: e.dma_start(out=d, in_=s)],
                             extra_wait=gw)
            wsplit_done = (wslot.sem, ldv)

            if kind in ("q", "k"):
                for ec in range(DC):
                    bank = oc_rot[0] % 4
                    oc_rot[0] += 1
                    p.wait2("tensor", bank_free[bank])
                    if ec == 0:
                        p.wait2("tensor", wsplit_done)
                        if iw == 0:
                            p.wait2("tensor", xn_ready)
                        wslot.set_free(None)
                    pbk = pb4[bank]
                    for dc in range(DC):
                        lwh = Wh[:, WB + dc * 1024 + ec * 128: WB + dc * 1024 + ec * 128 + 128]
                        lwl = Wl[:, WB + dc * 1024 + ec * 128: WB + dc * 1024 + ec * 128 + 128]
                        p.ins("tensor", lambda e, lwh=lwh, dc=dc, pbk=pbk:
                              e.matmul(pbk[:, :], lwh, xn_h[:, dc * TT:(dc + 1) * TT],
                                       start=(dc == 0), stop=False))
                        p.ins("tensor", lambda e, lwh=lwh, dc=dc, pbk=pbk:
                              e.matmul(pbk[:, :], lwh, xn_l[:, dc * TT:(dc + 1) * TT],
                                       start=False, stop=False))
                        p.ins("tensor", lambda e, lwl=lwl, dc=dc, pbk=pbk,
                              last=(dc == DC - 1):
                              e.matmul(pbk[:, :], lwl, xn_h[:, dc * TT:(dc + 1) * TT],
                                       start=False, stop=last),
                              s_mm if dc == DC - 1 else None, 1)
                    bank_mmv[bank] = s_mm.n
                    k = ec % 2
                    stage_h = hb_h[:, k * TT:(k + 1) * TT]
                    stage_l = hb_l[:, k * TT:(k + 1) * TT]
                    p.wait("vector", s_mm, bank_mmv[bank])
                    p.wait2("vector", kq_stage_free[k])
                    p.ins("vector", lambda e, bank=bank, sh=stage_h:
                          e.tensor_copy(out=sh, in_=pb4[bank][:, :]), s_vv, 1)
                    p.wait("vector", s_vv, s_vv.n)
                    vv = p.ins("vector", lambda e, bank=bank, sh=stage_h, slo=stage_l:
                               e.scalar_tensor_tensor(out=slo, in0=pb4[bank][:, :],
                                                      scalar=1.0, in1=sh,
                                                      op0=OP.mult, op1=OP.subtract),
                               s_vv, 1)
                    bank_free[bank] = (s_vv, vv)
                    p.wait("scalar", s_vv, vv)
                    stv = None
                    for hl, stage in ((0, stage_h), (1, stage_l)):
                        if kind == "q":
                            dstd = q_dram[hl, 128 * ec:128 * (ec + 1), t * TT:(t + 1) * TT]
                        else:
                            dstd = kin[t][hl, 128 * ec:128 * (ec + 1), :]
                        stv = p.ins("scalar", lambda e, d=dstd, s=stage:
                                    e.dma_start(out=d, in_=s), s_pst[k], 16)
                    kq_stage_free[k] = (s_pst[k], stv)
            else:
                for tc in range(TT // 128):
                    p.wait2("tensor", state["pO_free"])
                    if tc == 0:
                        p.wait2("tensor", wsplit_done)
                    for dc in range(DC):
                        lxh = xn_h[:, dc * TT + tc * 128: dc * TT + tc * 128 + 128]
                        lxl = xn_l[:, dc * TT + tc * 128: dc * TT + tc * 128 + 128]
                        for eh in range(2):
                            rW = Wh[:, WB + dc * 1024 + eh * 512: WB + dc * 1024 + (eh + 1) * 512]
                            rWl = Wl[:, WB + dc * 1024 + eh * 512: WB + dc * 1024 + (eh + 1) * 512]
                            p.ins("tensor", lambda e, lxh=lxh, rW=rW, dc=dc, eh=eh:
                                  e.matmul(pO[:, eh * 512:(eh + 1) * 512], lxh, rW,
                                           start=(dc == 0), stop=False))
                            p.ins("tensor", lambda e, lxh=lxh, rWl=rWl, dc=dc, eh=eh:
                                  e.matmul(pO[:, eh * 512:(eh + 1) * 512], lxh, rWl,
                                           start=False, stop=False))
                            p.ins("tensor", lambda e, lxl=lxl, rW=rW, dc=dc, eh=eh,
                                  last=(dc == DC - 1):
                                  e.matmul(pO[:, eh * 512:(eh + 1) * 512], lxl, rW,
                                           start=False, stop=last),
                                  s_mm if (dc == DC - 1 and eh == 1) else None, 1)
                    vmm = s_mm.n
                    k = tc % 2
                    stage_h = hb_h[:, 2 * TT + k * 1024: 2 * TT + (k + 1) * 1024]
                    stage_l = hb_l[:, 2 * TT + k * 1024: 2 * TT + (k + 1) * 1024]
                    p.wait("vector", s_mm, vmm)
                    p.wait2("vector", v_stage_free[k])
                    p.ins("vector", lambda e, sh=stage_h: e.tensor_copy(out=sh, in_=pO[:, :]),
                          s_vv, 1)
                    p.wait("vector", s_vv, s_vv.n)
                    vv = p.ins("vector", lambda e, sh=stage_h, slo=stage_l:
                               e.scalar_tensor_tensor(out=slo, in0=pO[:, :], scalar=1.0,
                                                      in1=sh, op0=OP.mult, op1=OP.subtract),
                               s_vv, 1)
                    state["pO_free"] = (s_vv, vv)
                    p.wait("scalar", s_vv, vv)
                    stv = None
                    for hl, stage in ((0, stage_h), (1, stage_l)):
                        stv = p.ins("scalar",
                                    lambda e, hl=hl, tc=tc, s=stage, t=t:
                                    e.dma_start(out=vin[t][hl, 128 * tc:128 * (tc + 1), :],
                                                in_=s), s_vst[k], 16)
                    v_stage_free[k] = (s_vst[k], stv)
            wslot.set_free((s_mm, s_mm.n))
        p.wait("gpsimd", s_pst[0], s_pst[0].n)
        p.wait("gpsimd", s_pst[1], s_pst[1].n)
        p.wait("gpsimd", s_vst[0], s_vst[0].n)
        p.wait("gpsimd", s_vst[1], s_vst[1].n)
        p.ins("gpsimd", lambda e, t=t: e.collective_compute(
            "AllGather", OP.bypass, replica_groups=RGROUPS,
            ins=[kin[t][:, :, :]], outs=[kg[t][:, :, :]]), s_cc, 1)
        p.ins("gpsimd", lambda e, t=t: e.collective_compute(
            "AllGather", OP.bypass, replica_groups=RGROUPS,
            ins=[vin[t][:, :, :]], outs=[vg[t][:, :, :]]), s_cc, 1)
        return (s_vv, s_vv.n)

    with nc.sbuf_tensor("ha_h", [128, HC * TT], BF16) as ha_h, \
         nc.sbuf_tensor("ha_l", [128, HC * TT], BF16) as ha_l, \
         nc.sbuf_tensor("hb_h", [128, HC * TT], BF16) as hb_h, \
         nc.sbuf_tensor("hb_l", [128, HC * TT], BF16) as hb_l:

        stage_b("w1")
        stage_b("w2")

        def pre_layer_t0(nm):
            stage_b(nm)
            if nm == "h2":
                split_attn_weights()

        bufs1 = (ha_h, ha_l, hb_h, hb_l)
        for t in range(NT):
            first = (t == 0)
            done = ffn_tile(t, x_in, r_dram, first, True, bufs1,
                            pre_layer=pre_layer_t0 if first else None)
            ctx["r_guard"] = done
            if PHASES >= 2:
                qkv_tile(t, bufs1)
                ctx["r_guard"] = qkv_tile.r_free
        r1_stores = s_w3st.n

    dbg_gate = 0
    if DEBUG:
        dbg1 = nc.dram_tensor("dbg1", [D, TOK], F32, kind="ExternalOutput")
        p.wait("gpsimd", s_w3st, r1_stores)
        p.ins("gpsimd", lambda e: e.dma_start(out=dbg1[:, :], in_=r_dram[:, :]), s_g, 16)
        dbg_gate = s_g.n
        if PHASES >= 2:
            dbgq = nc.dram_tensor("dbgq", [2, D, TOK], BF16, kind="ExternalOutput")
            dbgk = nc.dram_tensor("dbgk", [NT, 4, D, TT], BF16, kind="ExternalOutput")
            dbgv = nc.dram_tensor("dbgv", [NT, 4, TT, D], BF16, kind="ExternalOutput")
            p.wait("gpsimd", s_cc, 2 * NT)
            p.wait("gpsimd", s_pst[0], s_pst[0].n)
            p.wait("gpsimd", s_pst[1], s_pst[1].n)
            p.ins("gpsimd", lambda e: e.dma_start(out=dbgq[:, :, :], in_=q_dram[:, :, :]),
                  s_g, 16)
            for t in range(NT):
                p.ins("gpsimd", lambda e, t=t: e.dma_start(out=dbgk[t], in_=kg[t][:, :, :]),
                      s_g, 16)
                p.ins("gpsimd", lambda e, t=t: e.dma_start(out=dbgv[t], in_=vg[t][:, :, :]),
                      s_g, 16)
            dbg_gate = s_g.n
        p.wait("gpsimd", s_g, dbg_gate)

    with nc.sbuf_tensor("Ssc", [128, 4096], F32) as Ssc, \
         nc.sbuf_tensor("Ph", [128, 4096], BF16) as Ph, \
         nc.sbuf_tensor("Pl", [128, 4096], BF16) as Pl, \
         nc.sbuf_tensor("PTh", [128, 4096], BF16) as PTh, \
         nc.sbuf_tensor("PTl", [128, 4096], BF16) as PTl, \
         nc.sbuf_tensor("Ksth", [128, 8192], BF16) as Ksth, \
         nc.sbuf_tensor("Kstl", [128, 8192], BF16) as Kstl, \
         nc.sbuf_tensor("Vsth", [128, 8192], BF16) as Vsth, \
         nc.sbuf_tensor("Vstl", [128, 8192], BF16) as Vstl, \
         nc.sbuf_tensor("Qt", [128, 4096], BF16) as Qt, \
         nc.sbuf_tensor("O32", [128, 1024], F32) as O32, \
         nc.sbuf_tensor("Rq", [128, 1024], F32) as Rq:

        p.wait("sync", s_mm, s_mm.n)
        p.wait("sync", s_vv, s_vv.n)
        p.wait("sync", s_vs, s_vs.n)
        p.wait("sync", s_pst[0], s_pst[0].n)
        p.wait("sync", s_pst[1], s_pst[1].n)
        p.wait("sync", s_vst[0], s_vst[0].n)
        p.wait("sync", s_vst[1], s_vst[1].n)
        p.wait("sync", s_w3st, s_w3st.n)
        if DEBUG:
            p.wait("sync", s_g, dbg_gate)

        pth_free = None
        if DEBUG and PHASES >= 3:
            dbgS = nc.dram_tensor("dbgS", [128, 4096], F32, kind="ExternalOutput")
            dbgO = nc.dram_tensor("dbgO", [128, 1024], F32, kind="ExternalOutput")
            dbgM = nc.dram_tensor("dbgM", [128, 4], F32, kind="ExternalOutput")

        for j in range(NQ if PHASES >= 3 else 0):
            jb = j % 2
            qof = jb * 2048
            qb = []
            for hl in range(2):
                src = q_dram[hl].rearrange("(ec p) q -> p ec q", p=128)[:, :,
                                                                        128 * j:128 * (j + 1)]
                dst = Qt[:, qof + hl * 1024: qof + (hl + 1) * 1024].rearrange(
                    "p (ec q) -> p ec q", q=128)
                qb.append(lambda e, s=src, d=dst: e.dma_start(out=d, in_=s))
            q_ready = qtS[jb].load("sync", qb)

            tmax = j // 4
            wp = 128 * (j % 4 + 1)
            slabs = []
            for tp in range(tmax + 1):
                w = TT if tp < tmax else wp
                for rr in range(2):
                    slabs.append((tp, rr, w))
            kvlen = sum(w for _, _, w in slabs)
            nkb = kvlen // 128

            qtS[jb].wait_loaded("tensor", q_ready)
            off = 0
            for si, (tp, rr, w) in enumerate(slabs):
                kb_ = si % 2
                ksh = kg[tp][2 * rr + 0][:, 0:w].rearrange("(ec p) k -> p ec k", p=128)
                ksl = kg[tp][2 * rr + 1][:, 0:w].rearrange("(ec p) k -> p ec k", p=128)
                kdh = Ksth[:, kb_ * 4096: kb_ * 4096 + DC * w].rearrange(
                    "p (ec k) -> p ec k", k=w)
                kdl = Kstl[:, kb_ * 4096: kb_ * 4096 + DC * w].rearrange(
                    "p (ec k) -> p ec k", k=w)
                lkv = kstS[kb_].load("sync",
                                     [lambda e, s=ksh, d=kdh: e.dma_start(out=d, in_=s),
                                      lambda e, s=ksl, d=kdl: e.dma_start(out=d, in_=s)],
                                     extra_wait=(s_cc, 2 * (tp + 1)))
                bank = oc_rot[0] % 4
                oc_rot[0] += 1
                p.wait2("tensor", bank_free[bank])
                kstS[kb_].wait_loaded("tensor", lkv)
                pbk = pb4[bank]
                for ec in range(DC):
                    qh = Qt[:, qof + ec * 128: qof + (ec + 1) * 128]
                    ql = Qt[:, qof + 1024 + ec * 128: qof + 1024 + (ec + 1) * 128]
                    rh = Ksth[:, kb_ * 4096 + ec * w: kb_ * 4096 + (ec + 1) * w]
                    rl = Kstl[:, kb_ * 4096 + ec * w: kb_ * 4096 + (ec + 1) * w]
                    p.ins("tensor", lambda e, qh=qh, rh=rh, ec=ec, pbk=pbk, w=w:
                          e.matmul(pbk[:, 0:w], qh, rh, start=(ec == 0), stop=False))
                    p.ins("tensor", lambda e, qh=qh, rl=rl, pbk=pbk, w=w:
                          e.matmul(pbk[:, 0:w], qh, rl, start=False, stop=False))
                    p.ins("tensor", lambda e, ql=ql, rh=rh, pbk=pbk, w=w,
                          last=(ec == DC - 1):
                          e.matmul(pbk[:, 0:w], ql, rh, start=False, stop=last),
                          s_mm if ec == DC - 1 else None, 1)
                bank_mmv[bank] = s_mm.n
                kstS[kb_].set_free((s_mm, s_mm.n))
                p.wait("vector", s_mm, bank_mmv[bank])
                vv = p.ins("vector", lambda e, bank=bank, off=off, w=w:
                           e.tensor_scalar(out=Ssc[:, off:off + w], in0=pb4[bank][:, 0:w],
                                           scalar1=1.0 / 32.0, scalar2=None, op0=OP.mult),
                           s_vv, 1)
                if tp == tmax:
                    mk = mask0 if rr == 0 else mask1
                    p.wait("vector", s_vv, s_vv.n)
                    vv = p.ins("vector", lambda e, off=off, w=w, mk=mk:
                               e.tensor_tensor(out=Ssc[:, off + w - 128:off + w],
                                               in0=Ssc[:, off + w - 128:off + w],
                                               in1=mk[:, :], op=OP.add), s_vv, 1)
                bank_free[bank] = (s_vv, vv)
                off += w

            p.wait("vector", s_vv, s_vv.n)
            p.ins("vector", lambda e, kvlen=kvlen: e.reduce_max(
                out=negmx[:, :], in_=Ssc[:, 0:kvlen], axis=AXX, negate=True), s_vv, 1)
            mxv = s_vv.n
            p.wait("scalar", s_vv, mxv)
            off = 0
            pings = [sq[0], sq[1], stg[0], stg[1]]
            pingF = [sqF[0], sqF[1], stF[0], stF[1]]
            for si, (tp, rr, w) in enumerate(slabs):
                k = si % 4
                pg = pings[k]
                p.wait2("scalar", pingF[k])
                sv = p.ins("scalar", lambda e, off=off, w=w, pg=pg, si=si:
                           e.activation(out=pg[:, 0:w], in_=Ssc[:, off:off + w],
                                        func=AF.Exp, bias=negmx[:, 0:1],
                                        accum_out=sume[:, si:si + 1]), s_vs, 1)
                p.wait("vector", s_vs, sv)
                p.ins("vector", lambda e, off=off, w=w, pg=pg:
                      e.tensor_copy(out=Ph[:, off:off + w], in_=pg[:, 0:w]), s_vv, 1)
                p.wait("vector", s_vv, s_vv.n)
                vv = p.ins("vector", lambda e, off=off, w=w, pg=pg:
                           e.tensor_tensor(out=Pl[:, off:off + w], in0=pg[:, 0:w],
                                           in1=Ph[:, off:off + w], op=OP.subtract), s_vv, 1)
                pingF[k] = (s_vv, vv)
                off += w
            sqF[0], sqF[1], stF[0], stF[1] = pingF
            p.wait("scalar", s_vs, s_vs.n)
            sv = p.ins("scalar", lambda e, ns=len(slabs): e.activation(
                out=vtmp[:, 0:ns], in_=sume[:, 0:ns], func=AF.Copy,
                accum_out=sumt[:, :]), s_vs, 1)
            p.wait("vector", s_vs, sv)
            p.ins("vector", lambda e: e.reciprocal(out=rs[:, :], in_=sumt[:, :]), s_vv, 1)
            p_split = s_vv.n
            if DEBUG and j == DBG_J:
                p.wait("scalar", s_vv, p_split)
                p.ins("scalar", lambda e: e.dma_start(out=dbgS[:, :], in_=Ssc[:, :]),
                      s_g, 16)
                p.ins("scalar", lambda e: e.dma_start(out=dbgM[:, 0:1], in_=negmx[:, :], allow_slow_non_contiguous=True),
                      s_g, 16)
                p.ins("scalar", lambda e: e.dma_start(out=dbgM[:, 1:2], in_=sumt[:, :], allow_slow_non_contiguous=True),
                      s_g, 16)
                p.ins("scalar", lambda e: e.dma_start(out=dbgM[:, 2:3], in_=rs[:, :], allow_slow_non_contiguous=True),
                      s_g, 16)
                p.wait("scalar", s_g, s_g.n)

            p.wait("tensor", s_vv, p_split)
            p.wait2("tensor", pth_free)
            for kb in range(nkb):
                for psrc, pdst in ((Ph, PTh), (Pl, PTl)):
                    b6 = (1 if psrc is Pl else 0) ^ (kb % 2)
                    p.wait2("tensor", pt_free[b6])
                    mv = p.ins("tensor", lambda e, kb=kb, psrc=psrc, b6=b6:
                               e.transpose(pt2[b6][:, 0:128],
                                           psrc[:, kb * 128:(kb + 1) * 128], eye_b[:, :]),
                               s_mm, 1)
                    p.wait("scalar", s_mm, mv)
                    sv = p.ins("scalar", lambda e, kb=kb, pdst=pdst, b6=b6:
                               e.copy(out=pdst[:, kb * 128:(kb + 1) * 128],
                                      in_=pt2[b6][:, 0:128]), s_vs, 1)
                    pt_free[b6] = (s_vs, sv)
            pt_done = s_vs.n

            p.wait("tensor", s_vs, pt_done)
            p.wait2("tensor", state["pO_free"])
            kb = 0
            for si, (tp, rr, w) in enumerate(slabs):
                vb_ = si % 2
                nkc = w // 128
                vsh = vg[tp][2 * rr + 0][0:w, :].rearrange("(kc p) e -> p kc e", p=128)
                vsl = vg[tp][2 * rr + 1][0:w, :].rearrange("(kc p) e -> p kc e", p=128)
                vdh = Vsth[:, vb_ * 4096: vb_ * 4096 + nkc * 1024].rearrange(
                    "p (kc e) -> p kc e", e=1024)
                vdl = Vstl[:, vb_ * 4096: vb_ * 4096 + nkc * 1024].rearrange(
                    "p (kc e) -> p kc e", e=1024)
                lvv = vstS[vb_].load("sync",
                                     [lambda e, s=vsh, d=vdh: e.dma_start(out=d, in_=s),
                                      lambda e, s=vsl, d=vdl: e.dma_start(out=d, in_=s)])
                vstS[vb_].wait_loaded("tensor", lvv)
                for kc in range(nkc):
                    pth_l = PTh[:, kb * 128:(kb + 1) * 128]
                    ptl_l = PTl[:, kb * 128:(kb + 1) * 128]
                    first_g = (kb == 0)
                    last_g = (kb == nkb - 1)
                    slab_final = (kc == nkc - 1)
                    for lhs_hi, passes in ((True, ("h0", "h1", "l0", "l1")),
                                           (False, ("h0", "h1"))):
                        lhs = pth_l if lhs_hi else ptl_l
                        for ps in passes:
                            hl_v = 0 if ps[0] == "h" else 1
                            eh = int(ps[1])
                            vsrc2 = Vsth if hl_v == 0 else Vstl
                            rhs = vsrc2[:, vb_ * 4096 + kc * 1024 + eh * 512:
                                        vb_ * 4096 + kc * 1024 + (eh + 1) * 512]
                            is_first = first_g and lhs_hi and ps in ("h0", "h1")
                            is_last = last_g and (not lhs_hi) and ps in ("h0", "h1")
                            fin = slab_final and (not lhs_hi) and ps == "h1"
                            p.ins("tensor",
                                  lambda e, lhs=lhs, rhs=rhs, eh=eh, fg=is_first,
                                  lg=is_last:
                                  e.matmul(pO[:, eh * 512:(eh + 1) * 512], lhs, rhs,
                                           start=fg, stop=lg),
                                  s_mm if fin else None, 1)
                    kb += 1
                vstS[vb_].set_free((s_mm, s_mm.n))
            av_mm = s_mm.n
            pth_free = (s_mm, av_mm)
            qtS[jb].set_free((s_mm, av_mm))

            p.wait("vector", s_mm, av_mm)
            p.wait("vector", s_ost, s_ost.n)
            p.wait("vector", s_vv, s_vv.n)
            p.ins("vector", lambda e: e.tensor_scalar(out=O32[:, :], in0=pO[:, :],
                                                      scalar1=rs[:, 0:1], scalar2=None,
                                                      op0=OP.mult), s_vv, 1)
            onorm = s_vv.n
            state["pO_free"] = (s_vv, onorm)
            dbg_o_gate = 0
            if DEBUG and j == DBG_J:
                p.wait("scalar", s_vv, onorm)
                p.ins("scalar", lambda e: e.dma_start(out=dbgO[:, :], in_=O32[:, :]),
                      s_g, 16)
                dbg_o_gate = s_g.n
                p.wait("scalar", s_g, s_g.n)
            rsrc = r_dram.ap().rearrange("(dc p) q -> p dc q", p=128)[:, :,
                                                                      128 * j:128 * (j + 1)]
            rdst = Rq.ap().rearrange("p (dc q) -> p dc q", q=128)
            ldr = rq_slot.load("sync", [lambda e, s=rsrc, d=rdst: e.dma_start(out=d, in_=s)])
            p.wait("tensor", s_vv, onorm)
            if DEBUG and j == DBG_J:
                p.wait("tensor", s_g, dbg_o_gate)
            for dc in range(DC):
                bank = oc_rot[0] % 4
                oc_rot[0] += 1
                p.wait2("tensor", bank_free[bank])
                mv = p.ins("tensor", lambda e, dc=dc, bank=bank:
                           e.transpose(pb4[bank][:, 0:128], O32[:, dc * 128:(dc + 1) * 128],
                                       eye_f[:, :]), s_mm, 1)
                p.wait("vector", s_mm, mv)
                if dc == 0:
                    rq_slot.wait_loaded("vector", ldr)
                vv = p.ins("vector", lambda e, dc=dc, bank=bank:
                           e.tensor_tensor(out=O32[:, dc * 128:(dc + 1) * 128],
                                           in0=pb4[bank][:, 0:128],
                                           in1=Rq[:, dc * 128:(dc + 1) * 128], op=OP.add),
                           s_vv, 1)
                bank_free[bank] = (s_vv, vv)
            r2v = s_vv.n
            rq_slot.set_free((s_vv, r2v))
            p.wait("scalar", s_vv, r2v)
            if DEBUG and j == 0:
                p.wait("scalar", s_g, dbg_gate)
            osrc = O32.ap().rearrange("p (dc q) -> p dc q", q=128)
            odst = r_dram.ap().rearrange("(dc p) q -> p dc q", p=128)[:, :,
                                                                      128 * j:128 * (j + 1)]
            p.ins("scalar", lambda e, s=osrc, d=odst: e.dma_start(out=d, in_=s), s_ost, 16)
        r2_stores = s_ost.n

    if DEBUG and PHASES >= 3:
        dbg2 = nc.dram_tensor("dbg2", [D, TOK], F32, kind="ExternalOutput")
        p.wait("gpsimd", s_ost, r2_stores)
        p.ins("gpsimd", lambda e: e.dma_start(out=dbg2[:, :], in_=r_dram[:, :]), s_g, 16)
        p.wait("gpsimd", s_g, s_g.n)

    with nc.sbuf_tensor("ha_h2", [128, HC * TT], BF16) as ha_h, \
         nc.sbuf_tensor("ha_l2", [128, HC * TT], BF16) as ha_l, \
         nc.sbuf_tensor("hb_h2", [128, HC * TT], BF16) as hb_h, \
         nc.sbuf_tensor("hb_l2", [128, HC * TT], BF16) as hb_l:
        ctx["r_guard"] = (s_vv, s_vv.n)
        ctx["rdram_ready"] = (s_ost, r2_stores)
        bufs2 = (ha_h, ha_l, hb_h, hb_l)
        for t in range(NT if PHASES >= 4 else 0):
            done = ffn_tile(t, r_dram, y_out, False, False, bufs2)
            ctx["r_guard"] = done

    p.wait("scalar", s_w3st, s_w3st.n)
    p.wait("scalar", s_ost, s_ost.n)

    with st:
        with nc.Block() as blk:
            @blk.tensor
            def _(e):
                for f in p.ops["tensor"]:
                    f(e)

            @blk.vector
            def _(e):
                for f in p.ops["vector"]:
                    f(e)

            @blk.scalar
            def _(e):
                for f in p.ops["scalar"]:
                    f(e)

            @blk.sync
            def _(e):
                for f in p.ops["sync"]:
                    f(e)

            @blk.gpsimd
            def _(e):
                for f in p.ops["gpsimd"]:
                    f(e)
    return nc


def _shard_inputs(inputs):
    x = np.asarray(inputs["L0_attn_x"])
    import ml_dtypes
    tri = np.where(np.tril(np.ones((128, 128), dtype=bool)), 0.0, NEG).astype(np.float32)
    eye_f32 = np.eye(128, dtype=np.float32)
    eye_bf = eye_f32.astype(ml_dtypes.bfloat16)
    ones_f32 = np.ones((128, 128), np.float32)
    zeros = np.zeros((128, 128), np.float32)
    full = np.full((128, 128), NEG, np.float32)
    common = {
        "w1": np.ascontiguousarray(inputs["L0_ffn_w1_w"], np.float32),
        "w2": np.ascontiguousarray(inputs["L0_ffn_w2_w"], np.float32),
        "h1": np.ascontiguousarray(inputs["L0_ffn_h1_w"], np.float32),
        "h2": np.ascontiguousarray(inputs["L0_ffn_h2_w"], np.float32),
        "w3": np.ascontiguousarray(inputs["L0_ffn_w3_w"], np.float32),
        "wq": np.ascontiguousarray(inputs["L0_attn_Wq"], np.float32),
        "wk": np.ascontiguousarray(inputs["L0_attn_Wk"], np.float32),
        "wv": np.ascontiguousarray(inputs["L0_attn_Wv"], np.float32),
        "s1": np.ascontiguousarray(np.asarray(inputs["L0_ffn_w1_s"], np.float32).reshape(-1, 128).T),
        "s2": np.ascontiguousarray(np.asarray(inputs["L0_ffn_w2_s"], np.float32).reshape(-1, 128).T),
        "s3": np.ascontiguousarray(np.asarray(inputs["L0_ffn_h1_s"], np.float32).reshape(-1, 128).T),
        "s4": np.ascontiguousarray(np.asarray(inputs["L0_ffn_h2_s"], np.float32).reshape(-1, 128).T),
        "s5": np.ascontiguousarray(np.asarray(inputs["L0_ffn_w3_s"], np.float32).reshape(-1, 128).T),
    }
    maps = []
    for c in range(8):
        b, par = divmod(c, 2)
        loc = x[b].reshape(32, 128, D)[par::2].reshape(TOK, D)
        m = dict(common)
        m["x"] = np.ascontiguousarray(loc.T)
        m["mask0"] = tri if par == 0 else zeros
        m["mask1"] = full if par == 0 else tri
        m["eyeb"] = eye_bf
        m["eyef"] = eye_f32
        m["ones"] = ones_f32
        maps.append(m)
    return maps


def kernel(**inputs):
    maps = _shard_inputs(inputs)
    nc = build_program()
    res = run_bass_kernel_spmd(nc, maps, core_ids=list(range(8)))
    out = np.empty((4, 4096, D), np.float32)
    for c in range(8):
        b, par = divmod(c, 2)
        y = np.asarray(res.results[c]["y"])
        out[b].reshape(32, 128, D)[par::2] = y.T.reshape(16, 128, D)
    return out


# version 11
# speedup vs baseline: 1.1217x; 1.1217x over previous
import numpy as np
from contextlib import ExitStack

import concourse.bass as bass
import concourse.mybir as mybir
from concourse.alu_op_type import AluOpType as OP
from concourse.bass_utils import run_bass_kernel_spmd

F32 = mybir.dt.float32
BF16 = mybir.dt.bfloat16
AF = mybir.ActivationFunctionType
AXX = mybir.AxisListType.X

D = 1024
HID = 4096
TOK = 2048
TT = 512
NT = TOK // TT
DC = D // 128
HC = HID // 128
NQ = TOK // 128
NEG = -1.0e9
EPS = 1e-5
RGROUPS = [[0, 1], [2, 3], [4, 5], [6, 7]]

DEBUG = False
PHASES = 4
DBG_J = 0


class Sem:
    __slots__ = ("h", "n")

    def __init__(self, h):
        self.h = h
        self.n = 0


class PB:
    ENGINES = ("tensor", "vector", "scalar", "sync", "gpsimd")

    def __init__(self, nc, st):
        self.nc = nc
        self.st = st
        self.ops = {e: [] for e in self.ENGINES}

    def sem(self, name):
        return Sem(self.st.enter_context(self.nc.semaphore(name)))

    def emit(self, eng, fn):
        self.ops[eng].append(fn)

    def wait(self, eng, sem, val=None):
        v = sem.n if val is None else val
        if v <= 0:
            return
        self.emit(eng, lambda e, h=sem.h, v=v: e.wait_ge(h, v))

    def wait2(self, eng, sv):
        if sv is None:
            return
        if isinstance(sv, list):
            for x in sv:
                self.wait2(eng, x)
            return
        if sv[0] == "full":
            self.wait(eng, sv[1], sv[1].n)
            return
        self.wait(eng, sv[0], sv[1])

    def ins(self, eng, build, inc=None, by=1):
        if inc is None:
            self.emit(eng, build)
            return None
        inc.n += by
        self.emit(eng, lambda e, b=build, h=inc.h, by=by: b(e).then_inc(h, by))
        return inc.n


class Slot:

    def __init__(self, p, name):
        self.p = p
        self.sem = p.sem(name)
        self.free = None

    def load(self, queue, builds, extra_wait=None):
        self.p.wait2(queue, self.free)
        self.free = None
        if extra_wait is not None:
            self.p.wait2(queue, extra_wait)
        for b in builds:
            self.p.ins(queue, b, self.sem, 16)
        return self.sem.n

    def wait_loaded(self, eng, val=None):
        self.p.wait(eng, self.sem, self.sem.n if val is None else val)

    def set_free(self, sv):
        self.free = sv


def build_program():
    nc = bass.Bass("TRN2", target_bir_lowering=False)
    st = ExitStack()
    p = PB(nc, st)

    x_in = nc.dram_tensor("x", [D, TOK], F32, kind="ExternalInput")
    wdr = {}
    for nm, shp in [("w1", [D, HID]), ("w2", [D, HID]), ("h1", [HID, HID]),
                    ("h2", [HID, HID]), ("w3", [HID, D]),
                    ("wq", [D, D]), ("wk", [D, D]), ("wv", [D, D])]:
        wdr[nm] = nc.dram_tensor(nm, shp, F32, kind="ExternalInput")
    sdr = {}
    for nm, n in [("s1", HID), ("s2", HID), ("s3", HID), ("s4", HID), ("s5", D)]:
        sdr[nm] = nc.dram_tensor(nm, [128, n // 128], F32, kind="ExternalInput")
    m0_in = nc.dram_tensor("mask0", [128, 128], F32, kind="ExternalInput")
    m1_in = nc.dram_tensor("mask1", [128, 128], F32, kind="ExternalInput")
    eyeb_in = nc.dram_tensor("eyeb", [128, 128], BF16, kind="ExternalInput")
    eyef_in = nc.dram_tensor("eyef", [128, 128], F32, kind="ExternalInput")
    ones_in = nc.dram_tensor("ones", [128, 128], F32, kind="ExternalInput")
    y_out = nc.dram_tensor("y", [D, TOK], F32, kind="ExternalOutput")

    wshape = {nm: list(wdr[nm].shape) for nm in ("w1", "w2", "h1", "h2", "w3")}
    wt = {nm: nc.dram_tensor("wt_" + nm, wshape[nm], BF16) for nm in wshape}
    wspl = {}
    for wnm in ("wq", "wk", "wv"):
        for hl in ("h", "l"):
            wspl[(wnm, hl)] = nc.dram_tensor(f"{wnm}_{hl}", [D, D], BF16)
    r_dram = nc.dram_tensor("r_dram", [D, TOK], F32)
    q_dram = nc.dram_tensor("q_dram", [2, D, TOK], BF16)
    kin = [nc.dram_tensor(f"kin{t}", [2, D, TT], BF16) for t in range(NT)]
    vin = [nc.dram_tensor(f"vin{t}", [2, TT, D], BF16) for t in range(NT)]
    kg = [nc.dram_tensor(f"kg{t}", [4, D, TT], BF16) for t in range(NT)]
    vg = [nc.dram_tensor(f"vg{t}", [4, TT, D], BF16) for t in range(NT)]

    sb = st.enter_context
    ones_f = sb(nc.sbuf_tensor("ones_f", [128, 128], F32))
    eye_b = sb(nc.sbuf_tensor("eye_b", [128, 128], BF16))
    eye_f = sb(nc.sbuf_tensor("eye_f", [128, 128], F32))
    mask0 = sb(nc.sbuf_tensor("mask0s", [128, 128], F32))
    mask1 = sb(nc.sbuf_tensor("mask1s", [128, 128], F32))
    s1s = sb(nc.sbuf_tensor("s1s", [128, HC], F32))
    s2s = sb(nc.sbuf_tensor("s2s", [128, HC], F32))
    s3s = sb(nc.sbuf_tensor("s3s", [128, HC], F32))
    s4s = sb(nc.sbuf_tensor("s4s", [128, HC], F32))
    s5s = sb(nc.sbuf_tensor("s5s", [128, DC], F32))
    rstd = sb(nc.sbuf_tensor("rstd", [128, TT], F32))
    vtmp = sb(nc.sbuf_tensor("vtmp", [128, TT], F32))
    negmx = sb(nc.sbuf_tensor("negmx", [128, 1], F32))
    sume = sb(nc.sbuf_tensor("sume", [128, 8], F32))
    sumt = sb(nc.sbuf_tensor("sumt", [128, 1], F32))
    rs = sb(nc.sbuf_tensor("rs", [128, 1], F32))
    sq = [sb(nc.sbuf_tensor(f"sq{i}", [128, TT], F32)) for i in range(2)]
    stg = [sb(nc.sbuf_tensor(f"stg{i}", [128, TT], F32)) for i in range(2)]
    xn_h = sb(nc.sbuf_tensor("xn_h", [128, DC * TT], BF16))
    xn_l = sb(nc.sbuf_tensor("xn_l", [128, DC * TT], BF16))
    R = sb(nc.sbuf_tensor("R", [128, DC * TT], F32))
    ws = sb(nc.sbuf_tensor("ws", [128, 16384], BF16))

    pb4 = [sb(nc.psum_tensor(f"pb{i}", [128, 512], F32)) for i in range(4)]
    pO = sb(nc.psum_tensor("pO", [128, 1024], F32))
    pt2 = [sb(nc.psum_tensor(f"pt{i}", [128, 1024], BF16)) for i in range(2)]

    s_g = p.sem("s_g")
    s_cc = p.sem("s_cc")
    s_mm = p.sem("s_mm")
    s_vv = p.sem("s_vv")
    s_vs = p.sem("s_vs")
    s_w3st = p.sem("s_w3st")
    s_pst = [p.sem("s_pst0"), p.sem("s_pst1")]
    s_vst = [p.sem("s_vst0"), p.sem("s_vst1")]
    s_ost = p.sem("s_ost")

    ms_slot = Slot(p, "ms")
    wsS = [Slot(p, f"ws{i}") for i in range(4)]
    sqS = [Slot(p, "sqs0"), Slot(p, "sqs1")]
    kstS = [Slot(p, "kst0"), Slot(p, "kst1")]
    vstS = [Slot(p, "vst0"), Slot(p, "vst1")]
    qtS = [Slot(p, "qt0"), Slot(p, "qt1")]
    r_slot = Slot(p, "r_sl")
    rq_slot = Slot(p, "rq_sl")
    wslot = Slot(p, "wsl")

    bank_free = {i: None for i in range(4)}
    bank_mmv = {i: 0 for i in range(4)}
    state = {"pO_free": None}
    pt_free = {0: None, 1: None}
    sqF = [None, None]
    stF = [None, None]
    oc_rot = [0]

    ms_builds = [lambda e: e.dma_start(out=mask0[:, :], in_=m0_in[:, :]),
                 lambda e: e.dma_start(out=mask1[:, :], in_=m1_in[:, :]),
                 lambda e: e.dma_start(out=eye_b[:, :], in_=eyeb_in[:, :]),
                 lambda e: e.dma_start(out=eye_f[:, :], in_=eyef_in[:, :]),
                 lambda e: e.dma_start(out=ones_f[:, :], in_=ones_in[:, :])]
    for s_t, nm in [(s1s, "s1"), (s2s, "s2"), (s3s, "s3"), (s4s, "s4"), (s5s, "s5")]:
        ms_builds.append(lambda e, o=s_t, i=sdr[nm]: e.dma_start(out=o[:, :], in_=i[:, :]))
    ms_slot.load("sync", ms_builds)

    ms_slot.wait_loaded("tensor")
    ms_slot.wait_loaded("scalar")
    ms_slot.wait_loaded("vector")
    p.ins("vector", lambda e: e.tensor_scalar(out=s5s[:, :], in0=s5s[:, :], scalar1=0.5,
                                              scalar2=None, op0=OP.mult), s_vv, 1)
    s5_ready = s_vv.n
    s_wsp = p.sem("s_wsp")

    xnpF = [None, None]

    def vec_convert(jobs):
        for ji, (sap, outs) in enumerate(jobs):
            k = ji % 2
            sqS[k].set_free(sqF[k])
            ldv = sqS[k].load("gpsimd",
                              [lambda e, s=sap, k=k: e.dma_start(out=sq[k][:, :], in_=s)])
            sqS[k].wait_loaded("vector", ldv)
            p.wait2("vector", xnpF[k])
            xh = xn_h[:, k * 512:(k + 1) * 512]
            xl = xn_l[:, k * 512:(k + 1) * 512]
            p.ins("vector", lambda e, k=k, xh=xh: e.tensor_copy(out=xh, in_=sq[k][:, :]),
                  s_vv, 1)
            if any(kind == "l" for _, kind in outs):
                p.wait("vector", s_vv, s_vv.n)
                p.ins("vector", lambda e, k=k, xh=xh, xl=xl:
                      e.tensor_tensor(out=xl, in0=sq[k][:, :], in1=xh, op=OP.subtract),
                      s_vv, 1)
            vv = s_vv.n
            sqF[k] = (s_vv, vv)
            p.wait("gpsimd", s_vv, vv)
            for dap, kind in outs:
                p.ins("gpsimd",
                      lambda e, d=dap, s=(xh if kind == "h" else xl):
                      e.dma_start(out=d, in_=s), s_wsp, 16)
            xnpF[k] = ("full", s_wsp)

    def split_attn_weights():
        p.wait("vector", s_mm, s_mm.n)
        jobs = []
        for wnm in ("wq", "wk", "wv"):
            for ch in range(16):
                dc2, e2 = divmod(ch, 2)
                sap = wdr[wnm][128 * dc2:128 * (dc2 + 1), 512 * e2:512 * (e2 + 1)]
                outs = [(wspl[(wnm, hl)][128 * dc2:128 * (dc2 + 1),
                                         512 * e2:512 * (e2 + 1)], hl)
                        for hl in ("h", "l")]
                jobs.append((sap, outs))
        vec_convert(jobs)


    cast_done = {}
    for nm in ("w1", "w2", "h1", "h2", "w3"):
        rows = wshape[nm][0]
        nch = 4
        step = rows // nch
        for c in range(nch):
            p.ins("gpsimd",
                  lambda e, nm=nm, a=c * step, b=(c + 1) * step:
                  e.dma_start(out=wt[nm][a:b, :], in_=wdr[nm][a:b, :]), s_g, 16)
        cast_done[nm] = s_g.n
        p.wait("gpsimd", s_g, s_g.n)

    def ln_emit(r_ready):
        p.wait2("tensor", state["pO_free"])
        p.wait2("tensor", r_ready)
        p.wait2("vector", r_ready)
        p.wait("vector", s_vv, s_vv.n)
        for ci in range(DC):
            p.ins("tensor",
                  lambda e, ci=ci: e.matmul(pO[:, 0:512], ones_f[:, :],
                                            R[:, ci * TT:(ci + 1) * TT],
                                            start=(ci == 0), stop=(ci == DC - 1)),
                  s_mm if ci == DC - 1 else None, 1)
        mean_mm = s_mm.n
        p.wait("vector", s_mm, mean_mm)
        for ci in range(DC):
            k = ci % 2
            p.wait2("vector", sqF[k])
            vv = p.ins("vector",
                       lambda e, ci=ci, k=k: e.scalar_tensor_tensor(
                           out=sq[k][:, :], in0=pO[:, 0:512], scalar=-1.0 / D,
                           in1=R[:, ci * TT:(ci + 1) * TT], op0=OP.mult, op1=OP.add),
                       s_vv, 1)
            p.wait("scalar", s_vv, vv)
            p.wait2("scalar", stF[k])
            sv = p.ins("scalar", lambda e, k=k: e.square(stg[k][:, :], sq[k][:, :]), s_vs, 1)
            sqF[k] = (s_vs, sv)
            p.wait("tensor", s_vs, sv)
            mv = p.ins("tensor",
                       lambda e, ci=ci, k=k: e.matmul(pO[:, 512:1024], ones_f[:, :],
                                                      stg[k][:, :],
                                                      start=(ci == 0), stop=(ci == DC - 1)),
                       s_mm, 1)
            stF[k] = (s_mm, mv)
        var_mm = s_mm.n
        p.wait("vector", s_mm, var_mm)
        p.ins("vector", lambda e: e.tensor_scalar(out=vtmp[:, :], in0=pO[:, 512:1024],
                                                  scalar1=1.0 / D, scalar2=EPS,
                                                  op0=OP.mult, op1=OP.add), s_vv, 1)
        p.wait("scalar", s_vv, s_vv.n)
        p.ins("scalar", lambda e: e.activation(out=rstd[:, :], in_=vtmp[:, :], func=AF.Sqrt),
              s_vs, 1)
        p.wait("vector", s_vs, s_vs.n)
        p.ins("vector", lambda e: e.reciprocal(out=rstd[:, :], in_=rstd[:, :]), s_vv, 1)
        pings = [sq[0], sq[1], stg[0], stg[1]]
        pingF = [sqF[0], sqF[1], stF[0], stF[1]]
        for ci in range(DC):
            k = ci % 4
            pg = pings[k]
            p.wait2("vector", pingF[k])
            p.ins("vector",
                  lambda e, ci=ci, pg=pg: e.scalar_tensor_tensor(
                      out=pg[:, :], in0=pO[:, 0:512], scalar=-1.0 / D,
                      in1=R[:, ci * TT:(ci + 1) * TT], op0=OP.mult, op1=OP.add), s_vv, 1)
            p.wait("vector", s_vv, s_vv.n)
            p.ins("vector",
                  lambda e, pg=pg: e.tensor_tensor(out=pg[:, :], in0=pg[:, :],
                                                   in1=rstd[:, :], op=OP.mult), s_vv, 1)
            p.wait("vector", s_vv, s_vv.n)
            p.ins("vector",
                  lambda e, ci=ci, pg=pg: e.tensor_copy(out=xn_h[:, ci * TT:(ci + 1) * TT],
                                                        in_=pg[:, :]), s_vv, 1)
            p.wait("vector", s_vv, s_vv.n)
            vv = p.ins("vector",
                       lambda e, ci=ci, pg=pg: e.tensor_tensor(
                           out=xn_l[:, ci * TT:(ci + 1) * TT], in0=pg[:, :],
                           in1=xn_h[:, ci * TT:(ci + 1) * TT], op=OP.subtract), s_vv, 1)
            pingF[k] = (s_vv, vv)
        sqF[0], sqF[1], stF[0], stF[1] = pingF
        xn_ready = (s_vv, s_vv.n)
        state["pO_free"] = (s_vv, s_vv.n)
        return xn_ready

    def load_slab(nm, sl, n_ic, wo, slot_idx):
        slot = wsS[slot_idx]
        base = slot_idx * 4096
        src = wt[nm][:, sl * wo:(sl + 1) * wo].rearrange("(ci p) o -> p ci o", p=128)
        dst = ws[:, base:base + n_ic * wo].rearrange("p (ci o) -> p ci o", o=wo)
        gw = (s_g, cast_done[nm]) if load_slab.gate else None
        ldv = slot.load("sync", [lambda e, s=src, d=dst: e.dma_start(out=d, in_=s)],
                        extra_wait=gw)
        cols = {ci: base + ci * wo for ci in range(n_ic)}
        return slot, ldv, cols

    load_slab.gate = False

    def tlin_emit(nm, n_ic, n_o, rhs_h, rhs_l, rhs_ready, evac_fn, first_tile):
        load_slab.gate = first_tile
        wo = 4096 // n_ic
        for sl in range(n_o * 128 // wo):
            if first_tile and sl >= 2:
                load_slab.gate = False
            slot, ldv, cols = load_slab(nm, sl, n_ic, wo, sl % 4)
            for oc2 in range(wo // 128):
                oc = sl * (wo // 128) + oc2
                bank = oc_rot[0] % 4
                oc_rot[0] += 1
                p.wait2("tensor", bank_free[bank])
                if oc2 == 0:
                    slot.wait_loaded("tensor", ldv)
                    if sl == 0:
                        p.wait2("tensor", rhs_ready)
                pbk = pb4[bank]
                for ci in range(n_ic):
                    lh = ws[:, cols[ci] + oc2 * 128: cols[ci] + oc2 * 128 + 128]
                    p.ins("tensor",
                          lambda e, lh=lh, ci=ci, pbk=pbk:
                          e.matmul(pbk[:, :], lh, rhs_h[:, ci * TT:(ci + 1) * TT],
                                   start=(ci == 0), stop=False))
                    p.ins("tensor",
                          lambda e, lh=lh, ci=ci, pbk=pbk, last=(ci == n_ic - 1):
                          e.matmul(pbk[:, :], lh, rhs_l[:, ci * TT:(ci + 1) * TT],
                                   start=False, stop=last),
                          s_mm if ci == n_ic - 1 else None, 1)
                bank_mmv[bank] = s_mm.n
                evac_fn(oc, bank)
            slot.set_free((s_mm, s_mm.n))
        load_slab.gate = False

    ctx = {"r_guard": None, "rdram_ready": None}

    def ffn_tile(t, src_dram, dst_dram, first_tile, copy_back_R, bufs,
                 pre_layer=None):
        ha_h, ha_l, hb_h, hb_l = bufs
        p.wait("vector", s_w3st, s_w3st.n)
        p.wait("vector", s_ost, s_ost.n)
        p.wait("vector", s_pst[0], s_pst[0].n)
        p.wait("vector", s_pst[1], s_pst[1].n)
        p.wait("vector", s_vst[0], s_vst[0].n)
        p.wait("vector", s_vst[1], s_vst[1].n)
        src = src_dram.ap().rearrange("(dc p) tok -> p dc tok", p=128)[:, :, t * TT:(t + 1) * TT]
        dst = R.ap().rearrange("p (dc tok) -> p dc tok", tok=TT)
        extra = ctx["rdram_ready"] if src_dram is r_dram else None
        r_slot.set_free(ctx["r_guard"])
        ldv = r_slot.load("sync", [lambda e, s=src, d=dst: e.dma_start(out=d, in_=s)],
                          extra_wait=extra)
        xn_ready = ln_emit((r_slot.sem, ldv))

        silu_pend = {}
        load_slab.gate = first_tile
        for sl in range(HID // 512):
            if first_tile and sl >= 1:
                load_slab.gate = False
            slot1, ldv1, cols1 = load_slab("w1", sl, DC, 512, 2 * (sl % 2))
            slot2, ldv2, cols2 = load_slab("w2", sl, DC, 512, 2 * (sl % 2) + 1)
            for half in range(2):
                for nm in ("w1", "w2"):
                    slot, ldv, cols = (slot1, ldv1, cols1) if nm == "w1" else                                       (slot2, ldv2, cols2)
                    for oc2 in (2 * half, 2 * half + 1):
                        oc = sl * 4 + oc2
                        bank = oc_rot[0] % 4
                        oc_rot[0] += 1
                        p.wait2("tensor", bank_free[bank])
                        if oc2 == 2 * half and half == 0:
                            slot.wait_loaded("tensor", ldv)
                            if sl == 0 and nm == "w1":
                                p.wait2("tensor", xn_ready)
                        pbk = pb4[bank]
                        for ci in range(DC):
                            lh = ws[:, cols[ci] + oc2 * 128: cols[ci] + oc2 * 128 + 128]
                            p.ins("tensor",
                                  lambda e, lh=lh, ci=ci, pbk=pbk:
                                  e.matmul(pbk[:, :], lh, xn_h[:, ci * TT:(ci + 1) * TT],
                                           start=(ci == 0), stop=False))
                            p.ins("tensor",
                                  lambda e, lh=lh, ci=ci, pbk=pbk, last=(ci == DC - 1):
                                  e.matmul(pbk[:, :], lh, xn_l[:, ci * TT:(ci + 1) * TT],
                                           start=False, stop=last),
                                  s_mm if ci == DC - 1 else None, 1)
                        bank_mmv[bank] = s_mm.n
                        if nm == "w1":
                            k = oc % 2
                            p.wait("scalar", s_mm, bank_mmv[bank])
                            p.wait2("scalar", stF[k])
                            sv = p.ins("scalar",
                                       lambda e, oc=oc, bank=bank, k=k:
                                       e.activation(out=stg[k][:, :], in_=pb4[bank][:, :],
                                                    func=AF.Silu,
                                                    scale=s1s[:, oc:oc + 1]), s_vs, 1)
                            bank_free[bank] = (s_vs, sv)
                            silu_pend[oc] = (k, sv)
                        else:
                            k, sv = silu_pend.pop(oc)
                            p.wait("vector", s_mm, bank_mmv[bank])
                            p.wait("vector", s_vs, sv)
                            p.wait2("vector", sqF[k])
                            p.ins("vector",
                                  lambda e, oc=oc, bank=bank, k=k:
                                  e.scalar_tensor_tensor(out=sq[k][:, :],
                                                         in0=pb4[bank][:, :],
                                                         scalar=s2s[:, oc:oc + 1],
                                                         in1=stg[k][:, :],
                                                         op0=OP.mult, op1=OP.mult),
                                  s_vv, 1)
                            stF[k] = (s_vv, s_vv.n)
                            p.wait("vector", s_vv, s_vv.n)
                            p.ins("vector",
                                  lambda e, oc=oc, k=k:
                                  e.tensor_copy(out=ha_h[:, oc * TT:(oc + 1) * TT],
                                                in_=sq[k][:, :]), s_vv, 1)
                            p.wait("vector", s_vv, s_vv.n)
                            vv = p.ins("vector",
                                       lambda e, oc=oc, k=k: e.tensor_tensor(
                                           out=ha_l[:, oc * TT:(oc + 1) * TT],
                                           in0=sq[k][:, :],
                                           in1=ha_h[:, oc * TT:(oc + 1) * TT],
                                           op=OP.subtract), s_vv, 1)
                            bank_free[bank] = (s_vv, vv)
                            sqF[k] = (s_vv, vv)
            slot1.set_free((s_mm, s_mm.n))
            slot2.set_free((s_mm, s_mm.n))
        load_slab.gate = False
        ha_ready = (s_vv, s_vv.n)

        def evac_h(dst_h, dst_l, ssc):
            def f(oc, bank):
                k = oc % 2
                p.wait("scalar", s_mm, bank_mmv[bank])
                p.wait2("scalar", stF[k])
                sv = p.ins("scalar",
                           lambda e, oc=oc, bank=bank, k=k:
                           e.activation(out=stg[k][:, :], in_=pb4[bank][:, :], func=AF.Gelu,
                                        scale=ssc[:, oc:oc + 1]), s_vs, 1)
                bank_free[bank] = (s_vs, sv)
                p.wait("vector", s_vs, sv)
                p.ins("vector",
                      lambda e, oc=oc: e.tensor_copy(out=dst_h[:, oc * TT:(oc + 1) * TT],
                                                     in_=stg[k][:, :]), s_vv, 1)
                p.wait("vector", s_vv, s_vv.n)
                vv = p.ins("vector",
                           lambda e, oc=oc: e.tensor_tensor(
                               out=dst_l[:, oc * TT:(oc + 1) * TT], in0=stg[k][:, :],
                               in1=dst_h[:, oc * TT:(oc + 1) * TT], op=OP.subtract),
                           s_vv, 1)
                stF[k] = (s_vv, vv)
            return f

        if pre_layer:
            pre_layer("h1")
        tlin_emit("h1", HC, HC, ha_h, ha_l, ha_ready, evac_h(hb_h, hb_l, s3s), first_tile)
        hb_ready = (s_vv, s_vv.n)
        if pre_layer:
            pre_layer("h2")
        tlin_emit("h2", HC, HC, hb_h, hb_l, hb_ready, evac_h(ha_h, ha_l, s4s), first_tile)
        ha2_ready = (s_vv, s_vv.n)

        w3pings = [sq[0], sq[1], stg[0], stg[1]]
        w3F = [sqF[0], sqF[1], stF[0], stF[1]]

        def evac_w3(oc, bank):
            k = oc % 4
            pg = w3pings[k]
            p.wait("vector", s_mm, bank_mmv[bank])
            if first_tile and oc == 0:
                p.wait("vector", s_vv, s5_ready)
            p.wait2("vector", w3F[k])
            p.wait("vector", s_vv, s_vv.n)
            vv = p.ins("vector",
                       lambda e, oc=oc, bank=bank, pg=pg:
                       e.scalar_tensor_tensor(out=pg[:, :], in0=pb4[bank][:, :],
                                              scalar=s5s[:, oc:oc + 1],
                                              in1=R[:, oc * TT:(oc + 1) * TT],
                                              op0=OP.mult, op1=OP.add), s_vv, 1)
            bank_free[bank] = (s_vv, vv)
            p.wait("scalar", s_vv, vv)
            p.ins("scalar",
                  lambda e, oc=oc, pg=pg, t=t, dd=dst_dram:
                  e.dma_start(out=dd[oc * 128:(oc + 1) * 128, t * TT:(t + 1) * TT],
                              in_=pg[:, :]), s_w3st, 16)
            if copy_back_R:
                p.wait("vector", s_vv, s_vv.n)
                cv = p.ins("vector",
                           lambda e, oc=oc, pg=pg: e.tensor_copy(
                               out=R[:, oc * TT:(oc + 1) * TT], in_=pg[:, :]), s_vv, 1)
                w3F[k] = [(s_vv, cv), ("full", s_w3st)]
            else:
                w3F[k] = ("full", s_w3st)

        if pre_layer:
            pre_layer("w3")
        tlin_emit("w3", HC, DC, ha_h, ha_l, ha2_ready, evac_w3, first_tile)
        sqF[0], sqF[1], stF[0], stF[1] = w3F
        return (s_vv, s_vv.n)

    kq_stage_free = [None, None]
    v_stage_free = [None, None]

    def qkv_tile(t, bufs):
        ha_h, ha_l, hb_h, hb_l = bufs
        xn_ready = ln_emit((s_vv, s_vv.n))
        qkv_tile.r_free = [xn_ready, (s_mm, s_mm.n)]
        WB = 3072
        Wh = hb_h
        Wl = hb_l
        for iw, (wnm, kind) in enumerate((("wq", "q"), ("wk", "k"), ("wv", "v"))):
            srch = wspl[(wnm, "h")].ap().rearrange("(dc p) e -> p dc e", p=128)
            srcl = wspl[(wnm, "l")].ap().rearrange("(dc p) e -> p dc e", p=128)
            dsth = hb_h[:, WB:WB + 8192].rearrange("p (dc e) -> p dc e", e=1024)
            dstl = hb_l[:, WB:WB + 8192].rearrange("p (dc e) -> p dc e", e=1024)
            gw = [("full", s_wsp), (s_mm, s_mm.n)]
            ldv = wslot.load("sync",
                             [lambda e, s=srch, d=dsth: e.dma_start(out=d, in_=s),
                              lambda e, s=srcl, d=dstl: e.dma_start(out=d, in_=s)],
                             extra_wait=gw)
            wsplit_done = (wslot.sem, ldv)

            if kind in ("q", "k"):
                for ec in range(DC):
                    bank = oc_rot[0] % 4
                    oc_rot[0] += 1
                    p.wait2("tensor", bank_free[bank])
                    if ec == 0:
                        p.wait2("tensor", wsplit_done)
                        if iw == 0:
                            p.wait2("tensor", xn_ready)
                        wslot.set_free(None)
                    pbk = pb4[bank]
                    for dc in range(DC):
                        lwh = Wh[:, WB + dc * 1024 + ec * 128: WB + dc * 1024 + ec * 128 + 128]
                        lwl = Wl[:, WB + dc * 1024 + ec * 128: WB + dc * 1024 + ec * 128 + 128]
                        p.ins("tensor", lambda e, lwh=lwh, dc=dc, pbk=pbk:
                              e.matmul(pbk[:, :], lwh, xn_h[:, dc * TT:(dc + 1) * TT],
                                       start=(dc == 0), stop=False))
                        p.ins("tensor", lambda e, lwh=lwh, dc=dc, pbk=pbk:
                              e.matmul(pbk[:, :], lwh, xn_l[:, dc * TT:(dc + 1) * TT],
                                       start=False, stop=False))
                        p.ins("tensor", lambda e, lwl=lwl, dc=dc, pbk=pbk,
                              last=(dc == DC - 1):
                              e.matmul(pbk[:, :], lwl, xn_h[:, dc * TT:(dc + 1) * TT],
                                       start=False, stop=last),
                              s_mm if dc == DC - 1 else None, 1)
                    bank_mmv[bank] = s_mm.n
                    k = ec % 2
                    stage_h = hb_h[:, k * TT:(k + 1) * TT]
                    stage_l = hb_l[:, k * TT:(k + 1) * TT]
                    p.wait("vector", s_mm, bank_mmv[bank])
                    p.wait2("vector", kq_stage_free[k])
                    p.ins("vector", lambda e, bank=bank, sh=stage_h:
                          e.tensor_copy(out=sh, in_=pb4[bank][:, :]), s_vv, 1)
                    p.wait("vector", s_vv, s_vv.n)
                    vv = p.ins("vector", lambda e, bank=bank, sh=stage_h, slo=stage_l:
                               e.scalar_tensor_tensor(out=slo, in0=pb4[bank][:, :],
                                                      scalar=1.0, in1=sh,
                                                      op0=OP.mult, op1=OP.subtract),
                               s_vv, 1)
                    bank_free[bank] = (s_vv, vv)
                    p.wait("scalar", s_vv, vv)
                    stv = None
                    for hl, stage in ((0, stage_h), (1, stage_l)):
                        if kind == "q":
                            dstd = q_dram[hl, 128 * ec:128 * (ec + 1), t * TT:(t + 1) * TT]
                        else:
                            dstd = kin[t][hl, 128 * ec:128 * (ec + 1), :]
                        stv = p.ins("scalar", lambda e, d=dstd, s=stage:
                                    e.dma_start(out=d, in_=s), s_pst[k], 16)
                    kq_stage_free[k] = (s_pst[k], stv)
            else:
                for tc in range(TT // 128):
                    p.wait2("tensor", state["pO_free"])
                    if tc == 0:
                        p.wait2("tensor", wsplit_done)
                    for dc in range(DC):
                        lxh = xn_h[:, dc * TT + tc * 128: dc * TT + tc * 128 + 128]
                        lxl = xn_l[:, dc * TT + tc * 128: dc * TT + tc * 128 + 128]
                        for eh in range(2):
                            rW = Wh[:, WB + dc * 1024 + eh * 512: WB + dc * 1024 + (eh + 1) * 512]
                            rWl = Wl[:, WB + dc * 1024 + eh * 512: WB + dc * 1024 + (eh + 1) * 512]
                            p.ins("tensor", lambda e, lxh=lxh, rW=rW, dc=dc, eh=eh:
                                  e.matmul(pO[:, eh * 512:(eh + 1) * 512], lxh, rW,
                                           start=(dc == 0), stop=False))
                            p.ins("tensor", lambda e, lxh=lxh, rWl=rWl, dc=dc, eh=eh:
                                  e.matmul(pO[:, eh * 512:(eh + 1) * 512], lxh, rWl,
                                           start=False, stop=False))
                            p.ins("tensor", lambda e, lxl=lxl, rW=rW, dc=dc, eh=eh,
                                  last=(dc == DC - 1):
                                  e.matmul(pO[:, eh * 512:(eh + 1) * 512], lxl, rW,
                                           start=False, stop=last),
                                  s_mm if (dc == DC - 1 and eh == 1) else None, 1)
                    vmm = s_mm.n
                    k = tc % 2
                    stage_h = hb_h[:, 2 * TT + k * 1024: 2 * TT + (k + 1) * 1024]
                    stage_l = hb_l[:, 2 * TT + k * 1024: 2 * TT + (k + 1) * 1024]
                    p.wait("vector", s_mm, vmm)
                    p.wait2("vector", v_stage_free[k])
                    p.ins("vector", lambda e, sh=stage_h: e.tensor_copy(out=sh, in_=pO[:, :]),
                          s_vv, 1)
                    p.wait("vector", s_vv, s_vv.n)
                    vv = p.ins("vector", lambda e, sh=stage_h, slo=stage_l:
                               e.scalar_tensor_tensor(out=slo, in0=pO[:, :], scalar=1.0,
                                                      in1=sh, op0=OP.mult, op1=OP.subtract),
                               s_vv, 1)
                    state["pO_free"] = (s_vv, vv)
                    p.wait("scalar", s_vv, vv)
                    stv = None
                    for hl, stage in ((0, stage_h), (1, stage_l)):
                        stv = p.ins("scalar",
                                    lambda e, hl=hl, tc=tc, s=stage, t=t:
                                    e.dma_start(out=vin[t][hl, 128 * tc:128 * (tc + 1), :],
                                                in_=s), s_vst[k], 16)
                    v_stage_free[k] = (s_vst[k], stv)
            wslot.set_free((s_mm, s_mm.n))
        p.wait("gpsimd", s_pst[0], s_pst[0].n)
        p.wait("gpsimd", s_pst[1], s_pst[1].n)
        p.wait("gpsimd", s_vst[0], s_vst[0].n)
        p.wait("gpsimd", s_vst[1], s_vst[1].n)
        p.ins("gpsimd", lambda e, t=t: e.collective_compute(
            "AllGather", OP.bypass, replica_groups=RGROUPS,
            ins=[kin[t][:, :, :]], outs=[kg[t][:, :, :]]), s_cc, 1)
        p.ins("gpsimd", lambda e, t=t: e.collective_compute(
            "AllGather", OP.bypass, replica_groups=RGROUPS,
            ins=[vin[t][:, :, :]], outs=[vg[t][:, :, :]]), s_cc, 1)
        return (s_vv, s_vv.n)

    with nc.sbuf_tensor("ha_h", [128, HC * TT], BF16) as ha_h, \
         nc.sbuf_tensor("ha_l", [128, HC * TT], BF16) as ha_l, \
         nc.sbuf_tensor("hb_h", [128, HC * TT], BF16) as hb_h, \
         nc.sbuf_tensor("hb_l", [128, HC * TT], BF16) as hb_l:

        def pre_layer_t0(nm):
            if nm == "h2":
                split_attn_weights()

        bufs1 = (ha_h, ha_l, hb_h, hb_l)
        for t in range(NT):
            first = (t == 0)
            done = ffn_tile(t, x_in, r_dram, first, True, bufs1,
                            pre_layer=pre_layer_t0 if first else None)
            ctx["r_guard"] = done
            if PHASES >= 2:
                qkv_tile(t, bufs1)
                ctx["r_guard"] = qkv_tile.r_free
        r1_stores = s_w3st.n

    dbg_gate = 0
    if DEBUG:
        dbg1 = nc.dram_tensor("dbg1", [D, TOK], F32, kind="ExternalOutput")
        p.wait("gpsimd", s_w3st, r1_stores)
        p.ins("gpsimd", lambda e: e.dma_start(out=dbg1[:, :], in_=r_dram[:, :]), s_g, 16)
        dbg_gate = s_g.n
        if PHASES >= 2:
            dbgq = nc.dram_tensor("dbgq", [2, D, TOK], BF16, kind="ExternalOutput")
            dbgk = nc.dram_tensor("dbgk", [NT, 4, D, TT], BF16, kind="ExternalOutput")
            dbgv = nc.dram_tensor("dbgv", [NT, 4, TT, D], BF16, kind="ExternalOutput")
            p.wait("gpsimd", s_cc, 2 * NT)
            p.wait("gpsimd", s_pst[0], s_pst[0].n)
            p.wait("gpsimd", s_pst[1], s_pst[1].n)
            p.ins("gpsimd", lambda e: e.dma_start(out=dbgq[:, :, :], in_=q_dram[:, :, :]),
                  s_g, 16)
            for t in range(NT):
                p.ins("gpsimd", lambda e, t=t: e.dma_start(out=dbgk[t], in_=kg[t][:, :, :]),
                      s_g, 16)
                p.ins("gpsimd", lambda e, t=t: e.dma_start(out=dbgv[t], in_=vg[t][:, :, :]),
                      s_g, 16)
            dbg_gate = s_g.n
        p.wait("gpsimd", s_g, dbg_gate)

    with nc.sbuf_tensor("Ssc", [128, 4096], F32) as Ssc, \
         nc.sbuf_tensor("Ph", [128, 4096], BF16) as Ph, \
         nc.sbuf_tensor("Pl", [128, 4096], BF16) as Pl, \
         nc.sbuf_tensor("PTh", [128, 4096], BF16) as PTh, \
         nc.sbuf_tensor("PTl", [128, 4096], BF16) as PTl, \
         nc.sbuf_tensor("Ksth", [128, 8192], BF16) as Ksth, \
         nc.sbuf_tensor("Kstl", [128, 8192], BF16) as Kstl, \
         nc.sbuf_tensor("Vsth", [128, 8192], BF16) as Vsth, \
         nc.sbuf_tensor("Vstl", [128, 8192], BF16) as Vstl, \
         nc.sbuf_tensor("Qt", [128, 4096], BF16) as Qt, \
         nc.sbuf_tensor("O32", [128, 1024], F32) as O32, \
         nc.sbuf_tensor("Rq", [128, 1024], F32) as Rq:

        p.wait("sync", s_mm, s_mm.n)
        p.wait("sync", s_vv, s_vv.n)
        p.wait("sync", s_vs, s_vs.n)
        p.wait("sync", s_pst[0], s_pst[0].n)
        p.wait("sync", s_pst[1], s_pst[1].n)
        p.wait("sync", s_vst[0], s_vst[0].n)
        p.wait("sync", s_vst[1], s_vst[1].n)
        p.wait("sync", s_w3st, s_w3st.n)
        if DEBUG:
            p.wait("sync", s_g, dbg_gate)

        pth_free = None
        if DEBUG and PHASES >= 3:
            dbgS = nc.dram_tensor("dbgS", [128, 4096], F32, kind="ExternalOutput")
            dbgO = nc.dram_tensor("dbgO", [128, 1024], F32, kind="ExternalOutput")
            dbgM = nc.dram_tensor("dbgM", [128, 4], F32, kind="ExternalOutput")

        for j in range(NQ if PHASES >= 3 else 0):
            jb = j % 2
            qof = jb * 2048
            qb = []
            for hl in range(2):
                src = q_dram[hl].rearrange("(ec p) q -> p ec q", p=128)[:, :,
                                                                        128 * j:128 * (j + 1)]
                dst = Qt[:, qof + hl * 1024: qof + (hl + 1) * 1024].rearrange(
                    "p (ec q) -> p ec q", q=128)
                qb.append(lambda e, s=src, d=dst: e.dma_start(out=d, in_=s))
            q_ready = qtS[jb].load("sync", qb)

            tmax = j // 4
            wp = 128 * (j % 4 + 1)
            slabs = []
            for tp in range(tmax + 1):
                w = TT if tp < tmax else wp
                for rr in range(2):
                    slabs.append((tp, rr, w))
            kvlen = sum(w for _, _, w in slabs)
            nkb = kvlen // 128

            qtS[jb].wait_loaded("tensor", q_ready)
            off = 0
            for si, (tp, rr, w) in enumerate(slabs):
                kb_ = si % 2
                ksh = kg[tp][2 * rr + 0][:, 0:w].rearrange("(ec p) k -> p ec k", p=128)
                ksl = kg[tp][2 * rr + 1][:, 0:w].rearrange("(ec p) k -> p ec k", p=128)
                kdh = Ksth[:, kb_ * 4096: kb_ * 4096 + DC * w].rearrange(
                    "p (ec k) -> p ec k", k=w)
                kdl = Kstl[:, kb_ * 4096: kb_ * 4096 + DC * w].rearrange(
                    "p (ec k) -> p ec k", k=w)
                lkv = kstS[kb_].load("sync",
                                     [lambda e, s=ksh, d=kdh: e.dma_start(out=d, in_=s),
                                      lambda e, s=ksl, d=kdl: e.dma_start(out=d, in_=s)],
                                     extra_wait=(s_cc, 2 * (tp + 1)))
                bank = oc_rot[0] % 4
                oc_rot[0] += 1
                p.wait2("tensor", bank_free[bank])
                kstS[kb_].wait_loaded("tensor", lkv)
                pbk = pb4[bank]
                for ec in range(DC):
                    qh = Qt[:, qof + ec * 128: qof + (ec + 1) * 128]
                    ql = Qt[:, qof + 1024 + ec * 128: qof + 1024 + (ec + 1) * 128]
                    rh = Ksth[:, kb_ * 4096 + ec * w: kb_ * 4096 + (ec + 1) * w]
                    rl = Kstl[:, kb_ * 4096 + ec * w: kb_ * 4096 + (ec + 1) * w]
                    p.ins("tensor", lambda e, qh=qh, rh=rh, ec=ec, pbk=pbk, w=w:
                          e.matmul(pbk[:, 0:w], qh, rh, start=(ec == 0), stop=False))
                    p.ins("tensor", lambda e, qh=qh, rl=rl, pbk=pbk, w=w:
                          e.matmul(pbk[:, 0:w], qh, rl, start=False, stop=False))
                    p.ins("tensor", lambda e, ql=ql, rh=rh, pbk=pbk, w=w,
                          last=(ec == DC - 1):
                          e.matmul(pbk[:, 0:w], ql, rh, start=False, stop=last),
                          s_mm if ec == DC - 1 else None, 1)
                bank_mmv[bank] = s_mm.n
                kstS[kb_].set_free((s_mm, s_mm.n))
                p.wait("vector", s_mm, bank_mmv[bank])
                vv = p.ins("vector", lambda e, bank=bank, off=off, w=w:
                           e.tensor_scalar(out=Ssc[:, off:off + w], in0=pb4[bank][:, 0:w],
                                           scalar1=1.0 / 32.0, scalar2=None, op0=OP.mult),
                           s_vv, 1)
                if tp == tmax:
                    mk = mask0 if rr == 0 else mask1
                    p.wait("vector", s_vv, s_vv.n)
                    vv = p.ins("vector", lambda e, off=off, w=w, mk=mk:
                               e.tensor_tensor(out=Ssc[:, off + w - 128:off + w],
                                               in0=Ssc[:, off + w - 128:off + w],
                                               in1=mk[:, :], op=OP.add), s_vv, 1)
                bank_free[bank] = (s_vv, vv)
                off += w

            p.wait("vector", s_vv, s_vv.n)
            p.ins("vector", lambda e, kvlen=kvlen: e.reduce_max(
                out=negmx[:, :], in_=Ssc[:, 0:kvlen], axis=AXX, negate=True), s_vv, 1)
            mxv = s_vv.n
            p.wait("scalar", s_vv, mxv)
            off = 0
            pings = [sq[0], sq[1], stg[0], stg[1]]
            pingF = [sqF[0], sqF[1], stF[0], stF[1]]
            for si, (tp, rr, w) in enumerate(slabs):
                k = si % 4
                pg = pings[k]
                p.wait2("scalar", pingF[k])
                sv = p.ins("scalar", lambda e, off=off, w=w, pg=pg, si=si:
                           e.activation(out=pg[:, 0:w], in_=Ssc[:, off:off + w],
                                        func=AF.Exp, bias=negmx[:, 0:1],
                                        accum_out=sume[:, si:si + 1]), s_vs, 1)
                p.wait("vector", s_vs, sv)
                p.ins("vector", lambda e, off=off, w=w, pg=pg:
                      e.tensor_copy(out=Ph[:, off:off + w], in_=pg[:, 0:w]), s_vv, 1)
                p.wait("vector", s_vv, s_vv.n)
                vv = p.ins("vector", lambda e, off=off, w=w, pg=pg:
                           e.tensor_tensor(out=Pl[:, off:off + w], in0=pg[:, 0:w],
                                           in1=Ph[:, off:off + w], op=OP.subtract), s_vv, 1)
                pingF[k] = (s_vv, vv)
                off += w
            sqF[0], sqF[1], stF[0], stF[1] = pingF
            p.wait("scalar", s_vs, s_vs.n)
            sv = p.ins("scalar", lambda e, ns=len(slabs): e.activation(
                out=vtmp[:, 0:ns], in_=sume[:, 0:ns], func=AF.Copy,
                accum_out=sumt[:, :]), s_vs, 1)
            p.wait("vector", s_vs, sv)
            p.ins("vector", lambda e: e.reciprocal(out=rs[:, :], in_=sumt[:, :]), s_vv, 1)
            p_split = s_vv.n
            if DEBUG and j == DBG_J:
                p.wait("scalar", s_vv, p_split)
                p.ins("scalar", lambda e: e.dma_start(out=dbgS[:, :], in_=Ssc[:, :]),
                      s_g, 16)
                p.ins("scalar", lambda e: e.dma_start(out=dbgM[:, 0:1], in_=negmx[:, :], allow_slow_non_contiguous=True),
                      s_g, 16)
                p.ins("scalar", lambda e: e.dma_start(out=dbgM[:, 1:2], in_=sumt[:, :], allow_slow_non_contiguous=True),
                      s_g, 16)
                p.ins("scalar", lambda e: e.dma_start(out=dbgM[:, 2:3], in_=rs[:, :], allow_slow_non_contiguous=True),
                      s_g, 16)
                p.wait("scalar", s_g, s_g.n)

            p.wait("tensor", s_vv, p_split)
            p.wait2("tensor", pth_free)
            for kb in range(nkb):
                for psrc, pdst in ((Ph, PTh), (Pl, PTl)):
                    b6 = (1 if psrc is Pl else 0) ^ (kb % 2)
                    p.wait2("tensor", pt_free[b6])
                    mv = p.ins("tensor", lambda e, kb=kb, psrc=psrc, b6=b6:
                               e.transpose(pt2[b6][:, 0:128],
                                           psrc[:, kb * 128:(kb + 1) * 128], eye_b[:, :]),
                               s_mm, 1)
                    p.wait("scalar", s_mm, mv)
                    sv = p.ins("scalar", lambda e, kb=kb, pdst=pdst, b6=b6:
                               e.copy(out=pdst[:, kb * 128:(kb + 1) * 128],
                                      in_=pt2[b6][:, 0:128]), s_vs, 1)
                    pt_free[b6] = (s_vs, sv)
            pt_done = s_vs.n

            p.wait("tensor", s_vs, pt_done)
            p.wait2("tensor", state["pO_free"])
            kb = 0
            for si, (tp, rr, w) in enumerate(slabs):
                vb_ = si % 2
                nkc = w // 128
                vsh = vg[tp][2 * rr + 0][0:w, :].rearrange("(kc p) e -> p kc e", p=128)
                vsl = vg[tp][2 * rr + 1][0:w, :].rearrange("(kc p) e -> p kc e", p=128)
                vdh = Vsth[:, vb_ * 4096: vb_ * 4096 + nkc * 1024].rearrange(
                    "p (kc e) -> p kc e", e=1024)
                vdl = Vstl[:, vb_ * 4096: vb_ * 4096 + nkc * 1024].rearrange(
                    "p (kc e) -> p kc e", e=1024)
                lvv = vstS[vb_].load("sync",
                                     [lambda e, s=vsh, d=vdh: e.dma_start(out=d, in_=s),
                                      lambda e, s=vsl, d=vdl: e.dma_start(out=d, in_=s)])
                vstS[vb_].wait_loaded("tensor", lvv)
                for kc in range(nkc):
                    pth_l = PTh[:, kb * 128:(kb + 1) * 128]
                    ptl_l = PTl[:, kb * 128:(kb + 1) * 128]
                    first_g = (kb == 0)
                    last_g = (kb == nkb - 1)
                    slab_final = (kc == nkc - 1)
                    for lhs_hi, passes in ((True, ("h0", "h1", "l0", "l1")),
                                           (False, ("h0", "h1"))):
                        lhs = pth_l if lhs_hi else ptl_l
                        for ps in passes:
                            hl_v = 0 if ps[0] == "h" else 1
                            eh = int(ps[1])
                            vsrc2 = Vsth if hl_v == 0 else Vstl
                            rhs = vsrc2[:, vb_ * 4096 + kc * 1024 + eh * 512:
                                        vb_ * 4096 + kc * 1024 + (eh + 1) * 512]
                            is_first = first_g and lhs_hi and ps in ("h0", "h1")
                            is_last = last_g and (not lhs_hi) and ps in ("h0", "h1")
                            fin = slab_final and (not lhs_hi) and ps == "h1"
                            p.ins("tensor",
                                  lambda e, lhs=lhs, rhs=rhs, eh=eh, fg=is_first,
                                  lg=is_last:
                                  e.matmul(pO[:, eh * 512:(eh + 1) * 512], lhs, rhs,
                                           start=fg, stop=lg),
                                  s_mm if fin else None, 1)
                    kb += 1
                vstS[vb_].set_free((s_mm, s_mm.n))
            av_mm = s_mm.n
            pth_free = (s_mm, av_mm)
            qtS[jb].set_free((s_mm, av_mm))

            p.wait("vector", s_mm, av_mm)
            p.wait("vector", s_ost, s_ost.n)
            p.wait("vector", s_vv, s_vv.n)
            p.ins("vector", lambda e: e.tensor_scalar(out=O32[:, :], in0=pO[:, :],
                                                      scalar1=rs[:, 0:1], scalar2=None,
                                                      op0=OP.mult), s_vv, 1)
            onorm = s_vv.n
            state["pO_free"] = (s_vv, onorm)
            dbg_o_gate = 0
            if DEBUG and j == DBG_J:
                p.wait("scalar", s_vv, onorm)
                p.ins("scalar", lambda e: e.dma_start(out=dbgO[:, :], in_=O32[:, :]),
                      s_g, 16)
                dbg_o_gate = s_g.n
                p.wait("scalar", s_g, s_g.n)
            rsrc = r_dram.ap().rearrange("(dc p) q -> p dc q", p=128)[:, :,
                                                                      128 * j:128 * (j + 1)]
            rdst = Rq.ap().rearrange("p (dc q) -> p dc q", q=128)
            ldr = rq_slot.load("sync", [lambda e, s=rsrc, d=rdst: e.dma_start(out=d, in_=s)])
            p.wait("tensor", s_vv, onorm)
            if DEBUG and j == DBG_J:
                p.wait("tensor", s_g, dbg_o_gate)
            for dc in range(DC):
                bank = oc_rot[0] % 4
                oc_rot[0] += 1
                p.wait2("tensor", bank_free[bank])
                mv = p.ins("tensor", lambda e, dc=dc, bank=bank:
                           e.transpose(pb4[bank][:, 0:128], O32[:, dc * 128:(dc + 1) * 128],
                                       eye_f[:, :]), s_mm, 1)
                p.wait("vector", s_mm, mv)
                if dc == 0:
                    rq_slot.wait_loaded("vector", ldr)
                vv = p.ins("vector", lambda e, dc=dc, bank=bank:
                           e.tensor_tensor(out=O32[:, dc * 128:(dc + 1) * 128],
                                           in0=pb4[bank][:, 0:128],
                                           in1=Rq[:, dc * 128:(dc + 1) * 128], op=OP.add),
                           s_vv, 1)
                bank_free[bank] = (s_vv, vv)
            r2v = s_vv.n
            rq_slot.set_free((s_vv, r2v))
            p.wait("scalar", s_vv, r2v)
            if DEBUG and j == 0:
                p.wait("scalar", s_g, dbg_gate)
            osrc = O32.ap().rearrange("p (dc q) -> p dc q", q=128)
            odst = r_dram.ap().rearrange("(dc p) q -> p dc q", p=128)[:, :,
                                                                      128 * j:128 * (j + 1)]
            p.ins("scalar", lambda e, s=osrc, d=odst: e.dma_start(out=d, in_=s), s_ost, 16)
        r2_stores = s_ost.n

    if DEBUG and PHASES >= 3:
        dbg2 = nc.dram_tensor("dbg2", [D, TOK], F32, kind="ExternalOutput")
        p.wait("gpsimd", s_ost, r2_stores)
        p.ins("gpsimd", lambda e: e.dma_start(out=dbg2[:, :], in_=r_dram[:, :]), s_g, 16)
        p.wait("gpsimd", s_g, s_g.n)

    with nc.sbuf_tensor("ha_h2", [128, HC * TT], BF16) as ha_h, \
         nc.sbuf_tensor("ha_l2", [128, HC * TT], BF16) as ha_l, \
         nc.sbuf_tensor("hb_h2", [128, HC * TT], BF16) as hb_h, \
         nc.sbuf_tensor("hb_l2", [128, HC * TT], BF16) as hb_l:
        ctx["r_guard"] = (s_vv, s_vv.n)
        ctx["rdram_ready"] = (s_ost, r2_stores)
        bufs2 = (ha_h, ha_l, hb_h, hb_l)
        for t in range(NT if PHASES >= 4 else 0):
            done = ffn_tile(t, r_dram, y_out, False, False, bufs2)
            ctx["r_guard"] = done

    p.wait("scalar", s_w3st, s_w3st.n)
    p.wait("scalar", s_ost, s_ost.n)

    with st:
        with nc.Block() as blk:
            @blk.tensor
            def _(e):
                for f in p.ops["tensor"]:
                    f(e)

            @blk.vector
            def _(e):
                for f in p.ops["vector"]:
                    f(e)

            @blk.scalar
            def _(e):
                for f in p.ops["scalar"]:
                    f(e)

            @blk.sync
            def _(e):
                for f in p.ops["sync"]:
                    f(e)

            @blk.gpsimd
            def _(e):
                for f in p.ops["gpsimd"]:
                    f(e)
    return nc


def _shard_inputs(inputs):
    x = np.asarray(inputs["L0_attn_x"])
    import ml_dtypes
    tri = np.where(np.tril(np.ones((128, 128), dtype=bool)), 0.0, NEG).astype(np.float32)
    eye_f32 = np.eye(128, dtype=np.float32)
    eye_bf = eye_f32.astype(ml_dtypes.bfloat16)
    ones_f32 = np.ones((128, 128), np.float32)
    zeros = np.zeros((128, 128), np.float32)
    full = np.full((128, 128), NEG, np.float32)
    common = {
        "w1": np.ascontiguousarray(np.asarray(inputs["L0_ffn_w1_w"], np.float32).T),
        "w2": np.ascontiguousarray(np.asarray(inputs["L0_ffn_w2_w"], np.float32).T),
        "h1": np.ascontiguousarray(np.asarray(inputs["L0_ffn_h1_w"], np.float32).T),
        "h2": np.ascontiguousarray(np.asarray(inputs["L0_ffn_h2_w"], np.float32).T),
        "w3": np.ascontiguousarray(np.asarray(inputs["L0_ffn_w3_w"], np.float32).T),
        "wq": np.ascontiguousarray(inputs["L0_attn_Wq"], np.float32),
        "wk": np.ascontiguousarray(inputs["L0_attn_Wk"], np.float32),
        "wv": np.ascontiguousarray(inputs["L0_attn_Wv"], np.float32),
        "s1": np.ascontiguousarray(np.asarray(inputs["L0_ffn_w1_s"], np.float32).reshape(-1, 128).T),
        "s2": np.ascontiguousarray(np.asarray(inputs["L0_ffn_w2_s"], np.float32).reshape(-1, 128).T),
        "s3": np.ascontiguousarray(np.asarray(inputs["L0_ffn_h1_s"], np.float32).reshape(-1, 128).T),
        "s4": np.ascontiguousarray(np.asarray(inputs["L0_ffn_h2_s"], np.float32).reshape(-1, 128).T),
        "s5": np.ascontiguousarray(np.asarray(inputs["L0_ffn_w3_s"], np.float32).reshape(-1, 128).T),
    }
    maps = []
    for c in range(8):
        b, par = divmod(c, 2)
        loc = x[b].reshape(32, 128, D)[par::2].reshape(TOK, D)
        m = dict(common)
        m["x"] = np.ascontiguousarray(loc.T)
        m["mask0"] = tri if par == 0 else zeros
        m["mask1"] = full if par == 0 else tri
        m["eyeb"] = eye_bf
        m["eyef"] = eye_f32
        m["ones"] = ones_f32
        maps.append(m)
    return maps


def kernel(**inputs):
    maps = _shard_inputs(inputs)
    nc = build_program()
    res = run_bass_kernel_spmd(nc, maps, core_ids=list(range(8)))
    out = np.empty((4, 4096, D), np.float32)
    for c in range(8):
        b, par = divmod(c, 2)
        y = np.asarray(res.results[c]["y"])
        out[b].reshape(32, 128, D)[par::2] = y.T.reshape(16, 128, D)
    return out


# version 12
# speedup vs baseline: 1.2064x; 1.0755x over previous
import numpy as np
from contextlib import ExitStack

import concourse.bass as bass
import concourse.mybir as mybir
from concourse.alu_op_type import AluOpType as OP
from concourse.bass_utils import run_bass_kernel_spmd

F32 = mybir.dt.float32
BF16 = mybir.dt.bfloat16
AF = mybir.ActivationFunctionType
AXX = mybir.AxisListType.X

D = 1024
HID = 4096
TOK = 2048
TT = 512
NT = TOK // TT
DC = D // 128
HC = HID // 128
NQ = TOK // 128
NEG = -1.0e9
EPS = 1e-5
RGROUPS = [[0, 1], [2, 3], [4, 5], [6, 7]]

DEBUG = False
PHASES = 4
DBG_J = 0


class Sem:
    __slots__ = ("h", "n")

    def __init__(self, h):
        self.h = h
        self.n = 0


class PB:
    ENGINES = ("tensor", "vector", "scalar", "sync", "gpsimd")

    def __init__(self, nc, st):
        self.nc = nc
        self.st = st
        self.ops = {e: [] for e in self.ENGINES}

    def sem(self, name):
        return Sem(self.st.enter_context(self.nc.semaphore(name)))

    def emit(self, eng, fn):
        self.ops[eng].append(fn)

    def wait(self, eng, sem, val=None):
        v = sem.n if val is None else val
        if v <= 0:
            return
        self.emit(eng, lambda e, h=sem.h, v=v: e.wait_ge(h, v))

    def wait2(self, eng, sv):
        if sv is None:
            return
        if isinstance(sv, list):
            for x in sv:
                self.wait2(eng, x)
            return
        if sv[0] == "full":
            self.wait(eng, sv[1], sv[1].n)
            return
        self.wait(eng, sv[0], sv[1])

    def ins(self, eng, build, inc=None, by=1):
        if inc is None:
            self.emit(eng, build)
            return None
        inc.n += by
        self.emit(eng, lambda e, b=build, h=inc.h, by=by: b(e).then_inc(h, by))
        return inc.n


class Slot:

    def __init__(self, p, name):
        self.p = p
        self.sem = p.sem(name)
        self.free = None

    def load(self, queue, builds, extra_wait=None):
        self.p.wait2(queue, self.free)
        self.free = None
        if extra_wait is not None:
            self.p.wait2(queue, extra_wait)
        for b in builds:
            self.p.ins(queue, b, self.sem, 16)
        return self.sem.n

    def wait_loaded(self, eng, val=None):
        self.p.wait(eng, self.sem, self.sem.n if val is None else val)

    def set_free(self, sv):
        self.free = sv


def build_program():
    nc = bass.Bass("TRN2", target_bir_lowering=False)
    st = ExitStack()
    p = PB(nc, st)

    x_in = nc.dram_tensor("x", [D, TOK], F32, kind="ExternalInput")
    wdr = {}
    for nm, shp in [("w1", [D, HID]), ("w2", [D, HID]), ("h1", [HID, HID]),
                    ("h2", [HID, HID]), ("w3", [HID, D]),
                    ("wq", [D, D]), ("wk", [D, D]), ("wv", [D, D])]:
        wdr[nm] = nc.dram_tensor(nm, shp, F32, kind="ExternalInput")
    sdr = {}
    for nm, n in [("s1", HID), ("s2", HID), ("s3", HID), ("s4", HID), ("s5", D)]:
        sdr[nm] = nc.dram_tensor(nm, [128, n // 128], F32, kind="ExternalInput")
    m0_in = nc.dram_tensor("mask0", [128, 128], F32, kind="ExternalInput")
    m1_in = nc.dram_tensor("mask1", [128, 128], F32, kind="ExternalInput")
    eyeb_in = nc.dram_tensor("eyeb", [128, 128], BF16, kind="ExternalInput")
    eyef_in = nc.dram_tensor("eyef", [128, 128], F32, kind="ExternalInput")
    ones_in = nc.dram_tensor("ones", [128, 128], F32, kind="ExternalInput")
    y_out = nc.dram_tensor("y", [D, TOK], F32, kind="ExternalOutput")

    wshape = {nm: list(wdr[nm].shape) for nm in ("w1", "w2", "h1", "h2", "w3")}
    wt = {nm: nc.dram_tensor("wt_" + nm, wshape[nm], BF16) for nm in wshape}
    wspl = {}
    for wnm in ("wq", "wk", "wv"):
        for hl in ("h", "l"):
            wspl[(wnm, hl)] = nc.dram_tensor(f"{wnm}_{hl}", [D, D], BF16)
    r_dram = nc.dram_tensor("r_dram", [D, TOK], F32)
    q_dram = nc.dram_tensor("q_dram", [2, D, TOK], BF16)
    kin = [nc.dram_tensor(f"kin{t}", [2, D, TT], BF16) for t in range(NT)]
    vin = [nc.dram_tensor(f"vin{t}", [2, TT, D], BF16) for t in range(NT)]
    kg = [nc.dram_tensor(f"kg{t}", [4, D, TT], BF16) for t in range(NT)]
    vg = [nc.dram_tensor(f"vg{t}", [4, TT, D], BF16) for t in range(NT)]

    sb = st.enter_context
    ones_f = sb(nc.sbuf_tensor("ones_f", [128, 128], F32))
    eye_b = sb(nc.sbuf_tensor("eye_b", [128, 128], BF16))
    eye_f = sb(nc.sbuf_tensor("eye_f", [128, 128], F32))
    mask0 = sb(nc.sbuf_tensor("mask0s", [128, 128], F32))
    mask1 = sb(nc.sbuf_tensor("mask1s", [128, 128], F32))
    s1s = sb(nc.sbuf_tensor("s1s", [128, HC], F32))
    s2s = sb(nc.sbuf_tensor("s2s", [128, HC], F32))
    s3s = sb(nc.sbuf_tensor("s3s", [128, HC], F32))
    s4s = sb(nc.sbuf_tensor("s4s", [128, HC], F32))
    s5s = sb(nc.sbuf_tensor("s5s", [128, DC], F32))
    rstd = sb(nc.sbuf_tensor("rstd", [128, TT], F32))
    vtmp = sb(nc.sbuf_tensor("vtmp", [128, TT], F32))
    negmx = sb(nc.sbuf_tensor("negmx", [128, 1], F32))
    sume = sb(nc.sbuf_tensor("sume", [128, 8], F32))
    sumt = sb(nc.sbuf_tensor("sumt", [128, 1], F32))
    rs = sb(nc.sbuf_tensor("rs", [128, 1], F32))
    sq = [sb(nc.sbuf_tensor(f"sq{i}", [128, TT], F32)) for i in range(2)]
    stg = [sb(nc.sbuf_tensor(f"stg{i}", [128, TT], F32)) for i in range(2)]
    xn_h = sb(nc.sbuf_tensor("xn_h", [128, DC * TT], BF16))
    xn_l = sb(nc.sbuf_tensor("xn_l", [128, DC * TT], BF16))
    R = sb(nc.sbuf_tensor("R", [128, DC * TT], F32))
    ws = sb(nc.sbuf_tensor("ws", [128, 16384], BF16))

    pb4 = [sb(nc.psum_tensor(f"pb{i}", [128, 512], F32)) for i in range(4)]
    pO = sb(nc.psum_tensor("pO", [128, 1024], F32))
    pt2 = [sb(nc.psum_tensor(f"pt{i}", [128, 1024], BF16)) for i in range(2)]

    s_g = p.sem("s_g")
    s_cc = p.sem("s_cc")
    s_mm = p.sem("s_mm")
    s_vv = p.sem("s_vv")
    s_vs = p.sem("s_vs")
    s_w3st = p.sem("s_w3st")
    s_pst = [p.sem("s_pst0"), p.sem("s_pst1")]
    s_vst = [p.sem("s_vst0"), p.sem("s_vst1")]
    s_ost = p.sem("s_ost")

    ms_slot = Slot(p, "ms")
    wsS = [Slot(p, f"ws{i}") for i in range(4)]
    sqS = [Slot(p, "sqs0"), Slot(p, "sqs1")]
    kstS = [Slot(p, "kst0"), Slot(p, "kst1")]
    vstS = [Slot(p, "vst0"), Slot(p, "vst1")]
    qtS = [Slot(p, "qt0"), Slot(p, "qt1")]
    r_slot = Slot(p, "r_sl")
    rq_slot = Slot(p, "rq_sl")
    wslot = Slot(p, "wsl")

    bank_free = {i: None for i in range(4)}
    bank_mmv = {i: 0 for i in range(4)}
    state = {"pO_free": None}
    pt_free = {0: None, 1: None}
    sqF = [None, None]
    stF = [None, None]
    oc_rot = [0]

    ms_builds = [lambda e: e.dma_start(out=mask0[:, :], in_=m0_in[:, :]),
                 lambda e: e.dma_start(out=mask1[:, :], in_=m1_in[:, :]),
                 lambda e: e.dma_start(out=eye_b[:, :], in_=eyeb_in[:, :]),
                 lambda e: e.dma_start(out=eye_f[:, :], in_=eyef_in[:, :]),
                 lambda e: e.dma_start(out=ones_f[:, :], in_=ones_in[:, :])]
    for s_t, nm in [(s1s, "s1"), (s2s, "s2"), (s3s, "s3"), (s4s, "s4"), (s5s, "s5")]:
        ms_builds.append(lambda e, o=s_t, i=sdr[nm]: e.dma_start(out=o[:, :], in_=i[:, :]))
    ms_slot.load("sync", ms_builds)

    ms_slot.wait_loaded("tensor")
    ms_slot.wait_loaded("scalar")
    ms_slot.wait_loaded("vector")
    p.ins("vector", lambda e: e.tensor_scalar(out=s5s[:, :], in0=s5s[:, :], scalar1=0.5,
                                              scalar2=None, op0=OP.mult), s_vv, 1)
    s5_ready = s_vv.n
    s_wsp = p.sem("s_wsp")

    xnpF = [None, None]

    def vec_convert(jobs):
        for ji, (sap, outs) in enumerate(jobs):
            k = ji % 2
            sqS[k].set_free(sqF[k])
            ldv = sqS[k].load("gpsimd",
                              [lambda e, s=sap, k=k: e.dma_start(out=sq[k][:, :], in_=s)])
            sqS[k].wait_loaded("vector", ldv)
            p.wait2("vector", xnpF[k])
            xh = xn_h[:, k * 512:(k + 1) * 512]
            xl = xn_l[:, k * 512:(k + 1) * 512]
            p.ins("vector", lambda e, k=k, xh=xh: e.tensor_copy(out=xh, in_=sq[k][:, :]),
                  s_vv, 1)
            if any(kind == "l" for _, kind in outs):
                p.wait("vector", s_vv, s_vv.n)
                p.ins("vector", lambda e, k=k, xh=xh, xl=xl:
                      e.tensor_tensor(out=xl, in0=sq[k][:, :], in1=xh, op=OP.subtract),
                      s_vv, 1)
            vv = s_vv.n
            sqF[k] = (s_vv, vv)
            p.wait("gpsimd", s_vv, vv)
            for dap, kind in outs:
                p.ins("gpsimd",
                      lambda e, d=dap, s=(xh if kind == "h" else xl):
                      e.dma_start(out=d, in_=s), s_wsp, 16)
            xnpF[k] = ("full", s_wsp)

    def split_attn_weights():
        p.wait("vector", s_mm, s_mm.n)
        jobs = []
        for wnm in ("wq", "wk", "wv"):
            for ch in range(16):
                dc2, e2 = divmod(ch, 2)
                sap = wdr[wnm][128 * dc2:128 * (dc2 + 1), 512 * e2:512 * (e2 + 1)]
                outs = [(wspl[(wnm, hl)][128 * dc2:128 * (dc2 + 1),
                                         512 * e2:512 * (e2 + 1)], hl)
                        for hl in ("h", "l")]
                jobs.append((sap, outs))
        vec_convert(jobs)


    s_wcv = p.sem("s_wcv")
    rF = [None, None]
    xF = [None, None]
    for ch in range(16):
        rb, cb = divmod(ch, 2)
        k = ch % 2
        srcw = wdr["w1"][128 * rb:128 * (rb + 1), 2048 * cb:2048 * (cb + 1)]
        dstw = wt["w1"][128 * rb:128 * (rb + 1), 2048 * cb:2048 * (cb + 1)]
        rpg = R[:, 2048 * k:2048 * (k + 1)]
        xpg = xn_h[:, 2048 * k:2048 * (k + 1)]
        p.wait2("sync", rF[k])
        ldv = p.ins("sync", lambda e, s=srcw, d=rpg: e.dma_start(out=d, in_=s), s_wcv, 16)
        p.wait("vector", s_wcv, ldv)
        p.wait2("vector", xF[k])
        cvv = p.ins("vector", lambda e, s=rpg, d=xpg: e.tensor_copy(out=d, in_=s), s_vv, 1)
        rF[k] = (s_vv, cvv)
        p.wait("sync", s_vv, cvv)
        p.ins("sync", lambda e, s=xpg, d=dstw: e.dma_start(out=d, in_=s), s_wcv, 16)
        xF[k] = ("full", s_wcv)
    wcv_done = s_wcv.n
    p.wait("vector", s_wcv, wcv_done)

    cast_done = {"w1": None}
    for nm in ("w2", "h1", "h2", "w3"):
        rows = wshape[nm][0]
        nch = 4
        step = rows // nch
        for c in range(nch):
            p.ins("gpsimd",
                  lambda e, nm=nm, a=c * step, b=(c + 1) * step:
                  e.dma_start(out=wt[nm][a:b, :], in_=wdr[nm][a:b, :]), s_g, 16)
        cast_done[nm] = s_g.n
        p.wait("gpsimd", s_g, s_g.n)

    def ln_emit(r_ready):
        p.wait2("tensor", state["pO_free"])
        p.wait2("tensor", r_ready)
        p.wait2("vector", r_ready)
        p.wait("vector", s_vv, s_vv.n)
        for ci in range(DC):
            p.ins("tensor",
                  lambda e, ci=ci: e.matmul(pO[:, 0:512], ones_f[:, :],
                                            R[:, ci * TT:(ci + 1) * TT],
                                            start=(ci == 0), stop=(ci == DC - 1)),
                  s_mm if ci == DC - 1 else None, 1)
        mean_mm = s_mm.n
        p.wait("vector", s_mm, mean_mm)
        for ci in range(DC):
            k = ci % 2
            p.wait2("vector", sqF[k])
            vv = p.ins("vector",
                       lambda e, ci=ci, k=k: e.scalar_tensor_tensor(
                           out=sq[k][:, :], in0=pO[:, 0:512], scalar=-1.0 / D,
                           in1=R[:, ci * TT:(ci + 1) * TT], op0=OP.mult, op1=OP.add),
                       s_vv, 1)
            p.wait("scalar", s_vv, vv)
            p.wait2("scalar", stF[k])
            sv = p.ins("scalar", lambda e, k=k: e.square(stg[k][:, :], sq[k][:, :]), s_vs, 1)
            sqF[k] = (s_vs, sv)
            p.wait("tensor", s_vs, sv)
            mv = p.ins("tensor",
                       lambda e, ci=ci, k=k: e.matmul(pO[:, 512:1024], ones_f[:, :],
                                                      stg[k][:, :],
                                                      start=(ci == 0), stop=(ci == DC - 1)),
                       s_mm, 1)
            stF[k] = (s_mm, mv)
        var_mm = s_mm.n
        p.wait("vector", s_mm, var_mm)
        p.ins("vector", lambda e: e.tensor_scalar(out=vtmp[:, :], in0=pO[:, 512:1024],
                                                  scalar1=1.0 / D, scalar2=EPS,
                                                  op0=OP.mult, op1=OP.add), s_vv, 1)
        p.wait("scalar", s_vv, s_vv.n)
        p.ins("scalar", lambda e: e.activation(out=rstd[:, :], in_=vtmp[:, :], func=AF.Sqrt),
              s_vs, 1)
        p.wait("vector", s_vs, s_vs.n)
        p.ins("vector", lambda e: e.reciprocal(out=rstd[:, :], in_=rstd[:, :]), s_vv, 1)
        pings = [sq[0], sq[1], stg[0], stg[1]]
        pingF = [sqF[0], sqF[1], stF[0], stF[1]]
        for ci in range(DC):
            k = ci % 4
            pg = pings[k]
            p.wait2("vector", pingF[k])
            p.ins("vector",
                  lambda e, ci=ci, pg=pg: e.scalar_tensor_tensor(
                      out=pg[:, :], in0=pO[:, 0:512], scalar=-1.0 / D,
                      in1=R[:, ci * TT:(ci + 1) * TT], op0=OP.mult, op1=OP.add), s_vv, 1)
            p.wait("vector", s_vv, s_vv.n)
            p.ins("vector",
                  lambda e, pg=pg: e.tensor_tensor(out=pg[:, :], in0=pg[:, :],
                                                   in1=rstd[:, :], op=OP.mult), s_vv, 1)
            p.wait("vector", s_vv, s_vv.n)
            p.ins("vector",
                  lambda e, ci=ci, pg=pg: e.tensor_copy(out=xn_h[:, ci * TT:(ci + 1) * TT],
                                                        in_=pg[:, :]), s_vv, 1)
            p.wait("vector", s_vv, s_vv.n)
            vv = p.ins("vector",
                       lambda e, ci=ci, pg=pg: e.tensor_tensor(
                           out=xn_l[:, ci * TT:(ci + 1) * TT], in0=pg[:, :],
                           in1=xn_h[:, ci * TT:(ci + 1) * TT], op=OP.subtract), s_vv, 1)
            pingF[k] = (s_vv, vv)
        sqF[0], sqF[1], stF[0], stF[1] = pingF
        xn_ready = (s_vv, s_vv.n)
        state["pO_free"] = (s_vv, s_vv.n)
        return xn_ready

    def load_slab(nm, sl, n_ic, wo, slot_idx):
        slot = wsS[slot_idx]
        base = slot_idx * 4096
        src = wt[nm][:, sl * wo:(sl + 1) * wo].rearrange("(ci p) o -> p ci o", p=128)
        dst = ws[:, base:base + n_ic * wo].rearrange("p (ci o) -> p ci o", o=wo)
        if load_slab.gate:
            gw = (s_wcv, wcv_done) if nm == "w1" else (s_g, cast_done[nm])
        else:
            gw = None
        ldv = slot.load("sync", [lambda e, s=src, d=dst: e.dma_start(out=d, in_=s)],
                        extra_wait=gw)
        cols = {ci: base + ci * wo for ci in range(n_ic)}
        return slot, ldv, cols

    load_slab.gate = False

    def tlin_emit(nm, n_ic, n_o, rhs_h, rhs_l, rhs_ready, evac_fn, first_tile):
        load_slab.gate = first_tile
        wo = 4096 // n_ic
        for sl in range(n_o * 128 // wo):
            if first_tile and sl >= 2:
                load_slab.gate = False
            slot, ldv, cols = load_slab(nm, sl, n_ic, wo, sl % 4)
            for oc2 in range(wo // 128):
                oc = sl * (wo // 128) + oc2
                bank = oc_rot[0] % 4
                oc_rot[0] += 1
                p.wait2("tensor", bank_free[bank])
                if oc2 == 0:
                    slot.wait_loaded("tensor", ldv)
                    if sl == 0:
                        p.wait2("tensor", rhs_ready)
                pbk = pb4[bank]
                for ci in range(n_ic):
                    lh = ws[:, cols[ci] + oc2 * 128: cols[ci] + oc2 * 128 + 128]
                    p.ins("tensor",
                          lambda e, lh=lh, ci=ci, pbk=pbk:
                          e.matmul(pbk[:, :], lh, rhs_h[:, ci * TT:(ci + 1) * TT],
                                   start=(ci == 0), stop=False))
                    p.ins("tensor",
                          lambda e, lh=lh, ci=ci, pbk=pbk, last=(ci == n_ic - 1):
                          e.matmul(pbk[:, :], lh, rhs_l[:, ci * TT:(ci + 1) * TT],
                                   start=False, stop=last),
                          s_mm if ci == n_ic - 1 else None, 1)
                bank_mmv[bank] = s_mm.n
                evac_fn(oc, bank)
            slot.set_free((s_mm, s_mm.n))
        load_slab.gate = False

    ctx = {"r_guard": None, "rdram_ready": None}

    def ffn_tile(t, src_dram, dst_dram, first_tile, copy_back_R, bufs,
                 pre_layer=None):
        ha_h, ha_l, hb_h, hb_l = bufs
        p.wait("vector", s_w3st, s_w3st.n)
        p.wait("vector", s_ost, s_ost.n)
        p.wait("vector", s_pst[0], s_pst[0].n)
        p.wait("vector", s_pst[1], s_pst[1].n)
        p.wait("vector", s_vst[0], s_vst[0].n)
        p.wait("vector", s_vst[1], s_vst[1].n)
        src = src_dram.ap().rearrange("(dc p) tok -> p dc tok", p=128)[:, :, t * TT:(t + 1) * TT]
        dst = R.ap().rearrange("p (dc tok) -> p dc tok", tok=TT)
        extra = ctx["rdram_ready"] if src_dram is r_dram else None
        r_slot.set_free(ctx["r_guard"])
        ldv = r_slot.load("sync", [lambda e, s=src, d=dst: e.dma_start(out=d, in_=s)],
                          extra_wait=extra)
        xn_ready = ln_emit((r_slot.sem, ldv))

        silu_pend = {}
        load_slab.gate = first_tile
        for sl in range(HID // 512):
            if first_tile and sl >= 1:
                load_slab.gate = False
            slot1, ldv1, cols1 = load_slab("w1", sl, DC, 512, 2 * (sl % 2))
            slot2, ldv2, cols2 = load_slab("w2", sl, DC, 512, 2 * (sl % 2) + 1)
            for half in range(2):
                for nm in ("w1", "w2"):
                    slot, ldv, cols = (slot1, ldv1, cols1) if nm == "w1" else                                       (slot2, ldv2, cols2)
                    for oc2 in (2 * half, 2 * half + 1):
                        oc = sl * 4 + oc2
                        bank = oc_rot[0] % 4
                        oc_rot[0] += 1
                        p.wait2("tensor", bank_free[bank])
                        if oc2 == 2 * half and half == 0:
                            slot.wait_loaded("tensor", ldv)
                            if sl == 0 and nm == "w1":
                                p.wait2("tensor", xn_ready)
                        pbk = pb4[bank]
                        for ci in range(DC):
                            lh = ws[:, cols[ci] + oc2 * 128: cols[ci] + oc2 * 128 + 128]
                            p.ins("tensor",
                                  lambda e, lh=lh, ci=ci, pbk=pbk:
                                  e.matmul(pbk[:, :], lh, xn_h[:, ci * TT:(ci + 1) * TT],
                                           start=(ci == 0), stop=False))
                            p.ins("tensor",
                                  lambda e, lh=lh, ci=ci, pbk=pbk, last=(ci == DC - 1):
                                  e.matmul(pbk[:, :], lh, xn_l[:, ci * TT:(ci + 1) * TT],
                                           start=False, stop=last),
                                  s_mm if ci == DC - 1 else None, 1)
                        bank_mmv[bank] = s_mm.n
                        if nm == "w1":
                            k = oc % 2
                            p.wait("scalar", s_mm, bank_mmv[bank])
                            p.wait2("scalar", stF[k])
                            sv = p.ins("scalar",
                                       lambda e, oc=oc, bank=bank, k=k:
                                       e.activation(out=stg[k][:, :], in_=pb4[bank][:, :],
                                                    func=AF.Silu,
                                                    scale=s1s[:, oc:oc + 1]), s_vs, 1)
                            bank_free[bank] = (s_vs, sv)
                            silu_pend[oc] = (k, sv)
                        else:
                            k, sv = silu_pend.pop(oc)
                            p.wait("vector", s_mm, bank_mmv[bank])
                            p.wait("vector", s_vs, sv)
                            p.wait2("vector", sqF[k])
                            p.ins("vector",
                                  lambda e, oc=oc, bank=bank, k=k:
                                  e.scalar_tensor_tensor(out=sq[k][:, :],
                                                         in0=pb4[bank][:, :],
                                                         scalar=s2s[:, oc:oc + 1],
                                                         in1=stg[k][:, :],
                                                         op0=OP.mult, op1=OP.mult),
                                  s_vv, 1)
                            stF[k] = (s_vv, s_vv.n)
                            p.wait("vector", s_vv, s_vv.n)
                            p.ins("vector",
                                  lambda e, oc=oc, k=k:
                                  e.tensor_copy(out=ha_h[:, oc * TT:(oc + 1) * TT],
                                                in_=sq[k][:, :]), s_vv, 1)
                            p.wait("vector", s_vv, s_vv.n)
                            vv = p.ins("vector",
                                       lambda e, oc=oc, k=k: e.tensor_tensor(
                                           out=ha_l[:, oc * TT:(oc + 1) * TT],
                                           in0=sq[k][:, :],
                                           in1=ha_h[:, oc * TT:(oc + 1) * TT],
                                           op=OP.subtract), s_vv, 1)
                            bank_free[bank] = (s_vv, vv)
                            sqF[k] = (s_vv, vv)
            slot1.set_free((s_mm, s_mm.n))
            slot2.set_free((s_mm, s_mm.n))
        load_slab.gate = False
        ha_ready = (s_vv, s_vv.n)

        def evac_h(dst_h, dst_l, ssc):
            def f(oc, bank):
                k = oc % 2
                p.wait("scalar", s_mm, bank_mmv[bank])
                p.wait2("scalar", stF[k])
                sv = p.ins("scalar",
                           lambda e, oc=oc, bank=bank, k=k:
                           e.activation(out=stg[k][:, :], in_=pb4[bank][:, :], func=AF.Gelu,
                                        scale=ssc[:, oc:oc + 1]), s_vs, 1)
                bank_free[bank] = (s_vs, sv)
                p.wait("vector", s_vs, sv)
                p.ins("vector",
                      lambda e, oc=oc: e.tensor_copy(out=dst_h[:, oc * TT:(oc + 1) * TT],
                                                     in_=stg[k][:, :]), s_vv, 1)
                p.wait("vector", s_vv, s_vv.n)
                vv = p.ins("vector",
                           lambda e, oc=oc: e.tensor_tensor(
                               out=dst_l[:, oc * TT:(oc + 1) * TT], in0=stg[k][:, :],
                               in1=dst_h[:, oc * TT:(oc + 1) * TT], op=OP.subtract),
                           s_vv, 1)
                stF[k] = (s_vv, vv)
            return f

        if pre_layer:
            pre_layer("h1")
        tlin_emit("h1", HC, HC, ha_h, ha_l, ha_ready, evac_h(hb_h, hb_l, s3s), first_tile)
        hb_ready = (s_vv, s_vv.n)
        if pre_layer:
            pre_layer("h2")
        tlin_emit("h2", HC, HC, hb_h, hb_l, hb_ready, evac_h(ha_h, ha_l, s4s), first_tile)
        ha2_ready = (s_vv, s_vv.n)

        w3pings = [sq[0], sq[1], stg[0], stg[1]]
        w3F = [sqF[0], sqF[1], stF[0], stF[1]]

        def evac_w3(oc, bank):
            k = oc % 4
            pg = w3pings[k]
            p.wait("vector", s_mm, bank_mmv[bank])
            if first_tile and oc == 0:
                p.wait("vector", s_vv, s5_ready)
            p.wait2("vector", w3F[k])
            p.wait("vector", s_vv, s_vv.n)
            vv = p.ins("vector",
                       lambda e, oc=oc, bank=bank, pg=pg:
                       e.scalar_tensor_tensor(out=pg[:, :], in0=pb4[bank][:, :],
                                              scalar=s5s[:, oc:oc + 1],
                                              in1=R[:, oc * TT:(oc + 1) * TT],
                                              op0=OP.mult, op1=OP.add), s_vv, 1)
            bank_free[bank] = (s_vv, vv)
            p.wait("scalar", s_vv, vv)
            p.ins("scalar",
                  lambda e, oc=oc, pg=pg, t=t, dd=dst_dram:
                  e.dma_start(out=dd[oc * 128:(oc + 1) * 128, t * TT:(t + 1) * TT],
                              in_=pg[:, :]), s_w3st, 16)
            if copy_back_R:
                p.wait("vector", s_vv, s_vv.n)
                cv = p.ins("vector",
                           lambda e, oc=oc, pg=pg: e.tensor_copy(
                               out=R[:, oc * TT:(oc + 1) * TT], in_=pg[:, :]), s_vv, 1)
                w3F[k] = [(s_vv, cv), ("full", s_w3st)]
            else:
                w3F[k] = ("full", s_w3st)

        if pre_layer:
            pre_layer("w3")
        tlin_emit("w3", HC, DC, ha_h, ha_l, ha2_ready, evac_w3, first_tile)
        sqF[0], sqF[1], stF[0], stF[1] = w3F
        return (s_vv, s_vv.n)

    kq_stage_free = [None, None]
    v_stage_free = [None, None]

    def qkv_tile(t, bufs):
        ha_h, ha_l, hb_h, hb_l = bufs
        xn_ready = ln_emit((s_vv, s_vv.n))
        qkv_tile.r_free = [xn_ready, (s_mm, s_mm.n)]
        WB = 3072
        Wh = hb_h
        Wl = hb_l
        for iw, (wnm, kind) in enumerate((("wq", "q"), ("wk", "k"), ("wv", "v"))):
            srch = wspl[(wnm, "h")].ap().rearrange("(dc p) e -> p dc e", p=128)
            srcl = wspl[(wnm, "l")].ap().rearrange("(dc p) e -> p dc e", p=128)
            dsth = hb_h[:, WB:WB + 8192].rearrange("p (dc e) -> p dc e", e=1024)
            dstl = hb_l[:, WB:WB + 8192].rearrange("p (dc e) -> p dc e", e=1024)
            gw = [("full", s_wsp), (s_mm, s_mm.n)]
            ldv = wslot.load("sync",
                             [lambda e, s=srch, d=dsth: e.dma_start(out=d, in_=s),
                              lambda e, s=srcl, d=dstl: e.dma_start(out=d, in_=s)],
                             extra_wait=gw)
            wsplit_done = (wslot.sem, ldv)

            if kind in ("q", "k"):
                for ec in range(DC):
                    bank = oc_rot[0] % 4
                    oc_rot[0] += 1
                    p.wait2("tensor", bank_free[bank])
                    if ec == 0:
                        p.wait2("tensor", wsplit_done)
                        if iw == 0:
                            p.wait2("tensor", xn_ready)
                        wslot.set_free(None)
                    pbk = pb4[bank]
                    for dc in range(DC):
                        lwh = Wh[:, WB + dc * 1024 + ec * 128: WB + dc * 1024 + ec * 128 + 128]
                        lwl = Wl[:, WB + dc * 1024 + ec * 128: WB + dc * 1024 + ec * 128 + 128]
                        p.ins("tensor", lambda e, lwh=lwh, dc=dc, pbk=pbk:
                              e.matmul(pbk[:, :], lwh, xn_h[:, dc * TT:(dc + 1) * TT],
                                       start=(dc == 0), stop=False))
                        p.ins("tensor", lambda e, lwh=lwh, dc=dc, pbk=pbk:
                              e.matmul(pbk[:, :], lwh, xn_l[:, dc * TT:(dc + 1) * TT],
                                       start=False, stop=False))
                        p.ins("tensor", lambda e, lwl=lwl, dc=dc, pbk=pbk,
                              last=(dc == DC - 1):
                              e.matmul(pbk[:, :], lwl, xn_h[:, dc * TT:(dc + 1) * TT],
                                       start=False, stop=last),
                              s_mm if dc == DC - 1 else None, 1)
                    bank_mmv[bank] = s_mm.n
                    k = ec % 2
                    stage_h = hb_h[:, k * TT:(k + 1) * TT]
                    stage_l = hb_l[:, k * TT:(k + 1) * TT]
                    p.wait("vector", s_mm, bank_mmv[bank])
                    p.wait2("vector", kq_stage_free[k])
                    p.ins("vector", lambda e, bank=bank, sh=stage_h:
                          e.tensor_copy(out=sh, in_=pb4[bank][:, :]), s_vv, 1)
                    p.wait("vector", s_vv, s_vv.n)
                    vv = p.ins("vector", lambda e, bank=bank, sh=stage_h, slo=stage_l:
                               e.scalar_tensor_tensor(out=slo, in0=pb4[bank][:, :],
                                                      scalar=1.0, in1=sh,
                                                      op0=OP.mult, op1=OP.subtract),
                               s_vv, 1)
                    bank_free[bank] = (s_vv, vv)
                    p.wait("scalar", s_vv, vv)
                    stv = None
                    for hl, stage in ((0, stage_h), (1, stage_l)):
                        if kind == "q":
                            dstd = q_dram[hl, 128 * ec:128 * (ec + 1), t * TT:(t + 1) * TT]
                        else:
                            dstd = kin[t][hl, 128 * ec:128 * (ec + 1), :]
                        stv = p.ins("scalar", lambda e, d=dstd, s=stage:
                                    e.dma_start(out=d, in_=s), s_pst[k], 16)
                    kq_stage_free[k] = (s_pst[k], stv)
            else:
                for tc in range(TT // 128):
                    p.wait2("tensor", state["pO_free"])
                    if tc == 0:
                        p.wait2("tensor", wsplit_done)
                    for dc in range(DC):
                        lxh = xn_h[:, dc * TT + tc * 128: dc * TT + tc * 128 + 128]
                        lxl = xn_l[:, dc * TT + tc * 128: dc * TT + tc * 128 + 128]
                        for eh in range(2):
                            rW = Wh[:, WB + dc * 1024 + eh * 512: WB + dc * 1024 + (eh + 1) * 512]
                            rWl = Wl[:, WB + dc * 1024 + eh * 512: WB + dc * 1024 + (eh + 1) * 512]
                            p.ins("tensor", lambda e, lxh=lxh, rW=rW, dc=dc, eh=eh:
                                  e.matmul(pO[:, eh * 512:(eh + 1) * 512], lxh, rW,
                                           start=(dc == 0), stop=False))
                            p.ins("tensor", lambda e, lxh=lxh, rWl=rWl, dc=dc, eh=eh:
                                  e.matmul(pO[:, eh * 512:(eh + 1) * 512], lxh, rWl,
                                           start=False, stop=False))
                            p.ins("tensor", lambda e, lxl=lxl, rW=rW, dc=dc, eh=eh,
                                  last=(dc == DC - 1):
                                  e.matmul(pO[:, eh * 512:(eh + 1) * 512], lxl, rW,
                                           start=False, stop=last),
                                  s_mm if (dc == DC - 1 and eh == 1) else None, 1)
                    vmm = s_mm.n
                    k = tc % 2
                    stage_h = hb_h[:, 2 * TT + k * 1024: 2 * TT + (k + 1) * 1024]
                    stage_l = hb_l[:, 2 * TT + k * 1024: 2 * TT + (k + 1) * 1024]
                    p.wait("vector", s_mm, vmm)
                    p.wait2("vector", v_stage_free[k])
                    p.ins("vector", lambda e, sh=stage_h: e.tensor_copy(out=sh, in_=pO[:, :]),
                          s_vv, 1)
                    p.wait("vector", s_vv, s_vv.n)
                    vv = p.ins("vector", lambda e, sh=stage_h, slo=stage_l:
                               e.scalar_tensor_tensor(out=slo, in0=pO[:, :], scalar=1.0,
                                                      in1=sh, op0=OP.mult, op1=OP.subtract),
                               s_vv, 1)
                    state["pO_free"] = (s_vv, vv)
                    p.wait("scalar", s_vv, vv)
                    stv = None
                    for hl, stage in ((0, stage_h), (1, stage_l)):
                        stv = p.ins("scalar",
                                    lambda e, hl=hl, tc=tc, s=stage, t=t:
                                    e.dma_start(out=vin[t][hl, 128 * tc:128 * (tc + 1), :],
                                                in_=s), s_vst[k], 16)
                    v_stage_free[k] = (s_vst[k], stv)
            wslot.set_free((s_mm, s_mm.n))
        p.wait("gpsimd", s_pst[0], s_pst[0].n)
        p.wait("gpsimd", s_pst[1], s_pst[1].n)
        p.wait("gpsimd", s_vst[0], s_vst[0].n)
        p.wait("gpsimd", s_vst[1], s_vst[1].n)
        p.ins("gpsimd", lambda e, t=t: e.collective_compute(
            "AllGather", OP.bypass, replica_groups=RGROUPS,
            ins=[kin[t][:, :, :]], outs=[kg[t][:, :, :]]), s_cc, 1)
        p.ins("gpsimd", lambda e, t=t: e.collective_compute(
            "AllGather", OP.bypass, replica_groups=RGROUPS,
            ins=[vin[t][:, :, :]], outs=[vg[t][:, :, :]]), s_cc, 1)
        return (s_vv, s_vv.n)

    with nc.sbuf_tensor("ha_h", [128, HC * TT], BF16) as ha_h, \
         nc.sbuf_tensor("ha_l", [128, HC * TT], BF16) as ha_l, \
         nc.sbuf_tensor("hb_h", [128, HC * TT], BF16) as hb_h, \
         nc.sbuf_tensor("hb_l", [128, HC * TT], BF16) as hb_l:

        ctx["r_guard"] = (s_vv, s_vv.n)

        def pre_layer_t0(nm):
            if nm == "h2":
                split_attn_weights()

        bufs1 = (ha_h, ha_l, hb_h, hb_l)
        for t in range(NT):
            first = (t == 0)
            done = ffn_tile(t, x_in, r_dram, first, True, bufs1,
                            pre_layer=pre_layer_t0 if first else None)
            ctx["r_guard"] = done
            if PHASES >= 2:
                qkv_tile(t, bufs1)
                ctx["r_guard"] = qkv_tile.r_free
        r1_stores = s_w3st.n

    dbg_gate = 0
    if DEBUG:
        dbg1 = nc.dram_tensor("dbg1", [D, TOK], F32, kind="ExternalOutput")
        p.wait("gpsimd", s_w3st, r1_stores)
        p.ins("gpsimd", lambda e: e.dma_start(out=dbg1[:, :], in_=r_dram[:, :]), s_g, 16)
        dbg_gate = s_g.n
        if PHASES >= 2:
            dbgq = nc.dram_tensor("dbgq", [2, D, TOK], BF16, kind="ExternalOutput")
            dbgk = nc.dram_tensor("dbgk", [NT, 4, D, TT], BF16, kind="ExternalOutput")
            dbgv = nc.dram_tensor("dbgv", [NT, 4, TT, D], BF16, kind="ExternalOutput")
            p.wait("gpsimd", s_cc, 2 * NT)
            p.wait("gpsimd", s_pst[0], s_pst[0].n)
            p.wait("gpsimd", s_pst[1], s_pst[1].n)
            p.ins("gpsimd", lambda e: e.dma_start(out=dbgq[:, :, :], in_=q_dram[:, :, :]),
                  s_g, 16)
            for t in range(NT):
                p.ins("gpsimd", lambda e, t=t: e.dma_start(out=dbgk[t], in_=kg[t][:, :, :]),
                      s_g, 16)
                p.ins("gpsimd", lambda e, t=t: e.dma_start(out=dbgv[t], in_=vg[t][:, :, :]),
                      s_g, 16)
            dbg_gate = s_g.n
        p.wait("gpsimd", s_g, dbg_gate)

    with nc.sbuf_tensor("Ssc", [128, 4096], F32) as Ssc, \
         nc.sbuf_tensor("Ph", [128, 4096], BF16) as Ph, \
         nc.sbuf_tensor("Pl", [128, 4096], BF16) as Pl, \
         nc.sbuf_tensor("PTh", [128, 4096], BF16) as PTh, \
         nc.sbuf_tensor("PTl", [128, 4096], BF16) as PTl, \
         nc.sbuf_tensor("Ksth", [128, 8192], BF16) as Ksth, \
         nc.sbuf_tensor("Kstl", [128, 8192], BF16) as Kstl, \
         nc.sbuf_tensor("Vsth", [128, 8192], BF16) as Vsth, \
         nc.sbuf_tensor("Vstl", [128, 8192], BF16) as Vstl, \
         nc.sbuf_tensor("Qt", [128, 4096], BF16) as Qt, \
         nc.sbuf_tensor("O32", [128, 1024], F32) as O32, \
         nc.sbuf_tensor("Rq", [128, 1024], F32) as Rq:

        p.wait("sync", s_mm, s_mm.n)
        p.wait("sync", s_vv, s_vv.n)
        p.wait("sync", s_vs, s_vs.n)
        p.wait("sync", s_pst[0], s_pst[0].n)
        p.wait("sync", s_pst[1], s_pst[1].n)
        p.wait("sync", s_vst[0], s_vst[0].n)
        p.wait("sync", s_vst[1], s_vst[1].n)
        p.wait("sync", s_w3st, s_w3st.n)
        if DEBUG:
            p.wait("sync", s_g, dbg_gate)

        pth_free = None
        if DEBUG and PHASES >= 3:
            dbgS = nc.dram_tensor("dbgS", [128, 4096], F32, kind="ExternalOutput")
            dbgO = nc.dram_tensor("dbgO", [128, 1024], F32, kind="ExternalOutput")
            dbgM = nc.dram_tensor("dbgM", [128, 4], F32, kind="ExternalOutput")

        for j in range(NQ if PHASES >= 3 else 0):
            jb = j % 2
            qof = jb * 2048
            qb = []
            for hl in range(2):
                src = q_dram[hl].rearrange("(ec p) q -> p ec q", p=128)[:, :,
                                                                        128 * j:128 * (j + 1)]
                dst = Qt[:, qof + hl * 1024: qof + (hl + 1) * 1024].rearrange(
                    "p (ec q) -> p ec q", q=128)
                qb.append(lambda e, s=src, d=dst: e.dma_start(out=d, in_=s))
            q_ready = qtS[jb].load("sync", qb)

            tmax = j // 4
            wp = 128 * (j % 4 + 1)
            slabs = []
            for tp in range(tmax + 1):
                w = TT if tp < tmax else wp
                for rr in range(2):
                    slabs.append((tp, rr, w))
            kvlen = sum(w for _, _, w in slabs)
            nkb = kvlen // 128

            qtS[jb].wait_loaded("tensor", q_ready)
            off = 0
            for si, (tp, rr, w) in enumerate(slabs):
                kb_ = si % 2
                ksh = kg[tp][2 * rr + 0][:, 0:w].rearrange("(ec p) k -> p ec k", p=128)
                ksl = kg[tp][2 * rr + 1][:, 0:w].rearrange("(ec p) k -> p ec k", p=128)
                kdh = Ksth[:, kb_ * 4096: kb_ * 4096 + DC * w].rearrange(
                    "p (ec k) -> p ec k", k=w)
                kdl = Kstl[:, kb_ * 4096: kb_ * 4096 + DC * w].rearrange(
                    "p (ec k) -> p ec k", k=w)
                lkv = kstS[kb_].load("sync",
                                     [lambda e, s=ksh, d=kdh: e.dma_start(out=d, in_=s),
                                      lambda e, s=ksl, d=kdl: e.dma_start(out=d, in_=s)],
                                     extra_wait=(s_cc, 2 * (tp + 1)))
                bank = oc_rot[0] % 4
                oc_rot[0] += 1
                p.wait2("tensor", bank_free[bank])
                kstS[kb_].wait_loaded("tensor", lkv)
                pbk = pb4[bank]
                for ec in range(DC):
                    qh = Qt[:, qof + ec * 128: qof + (ec + 1) * 128]
                    ql = Qt[:, qof + 1024 + ec * 128: qof + 1024 + (ec + 1) * 128]
                    rh = Ksth[:, kb_ * 4096 + ec * w: kb_ * 4096 + (ec + 1) * w]
                    rl = Kstl[:, kb_ * 4096 + ec * w: kb_ * 4096 + (ec + 1) * w]
                    p.ins("tensor", lambda e, qh=qh, rh=rh, ec=ec, pbk=pbk, w=w:
                          e.matmul(pbk[:, 0:w], qh, rh, start=(ec == 0), stop=False))
                    p.ins("tensor", lambda e, qh=qh, rl=rl, pbk=pbk, w=w:
                          e.matmul(pbk[:, 0:w], qh, rl, start=False, stop=False))
                    p.ins("tensor", lambda e, ql=ql, rh=rh, pbk=pbk, w=w,
                          last=(ec == DC - 1):
                          e.matmul(pbk[:, 0:w], ql, rh, start=False, stop=last),
                          s_mm if ec == DC - 1 else None, 1)
                bank_mmv[bank] = s_mm.n
                kstS[kb_].set_free((s_mm, s_mm.n))
                p.wait("vector", s_mm, bank_mmv[bank])
                vv = p.ins("vector", lambda e, bank=bank, off=off, w=w:
                           e.tensor_scalar(out=Ssc[:, off:off + w], in0=pb4[bank][:, 0:w],
                                           scalar1=1.0 / 32.0, scalar2=None, op0=OP.mult),
                           s_vv, 1)
                if tp == tmax:
                    mk = mask0 if rr == 0 else mask1
                    p.wait("vector", s_vv, s_vv.n)
                    vv = p.ins("vector", lambda e, off=off, w=w, mk=mk:
                               e.tensor_tensor(out=Ssc[:, off + w - 128:off + w],
                                               in0=Ssc[:, off + w - 128:off + w],
                                               in1=mk[:, :], op=OP.add), s_vv, 1)
                bank_free[bank] = (s_vv, vv)
                off += w

            p.wait("vector", s_vv, s_vv.n)
            p.ins("vector", lambda e, kvlen=kvlen: e.reduce_max(
                out=negmx[:, :], in_=Ssc[:, 0:kvlen], axis=AXX, negate=True), s_vv, 1)
            mxv = s_vv.n
            p.wait("scalar", s_vv, mxv)
            off = 0
            pings = [sq[0], sq[1], stg[0], stg[1]]
            pingF = [sqF[0], sqF[1], stF[0], stF[1]]
            for si, (tp, rr, w) in enumerate(slabs):
                k = si % 4
                pg = pings[k]
                p.wait2("scalar", pingF[k])
                sv = p.ins("scalar", lambda e, off=off, w=w, pg=pg, si=si:
                           e.activation(out=pg[:, 0:w], in_=Ssc[:, off:off + w],
                                        func=AF.Exp, bias=negmx[:, 0:1],
                                        accum_out=sume[:, si:si + 1]), s_vs, 1)
                p.wait("vector", s_vs, sv)
                p.ins("vector", lambda e, off=off, w=w, pg=pg:
                      e.tensor_copy(out=Ph[:, off:off + w], in_=pg[:, 0:w]), s_vv, 1)
                p.wait("vector", s_vv, s_vv.n)
                vv = p.ins("vector", lambda e, off=off, w=w, pg=pg:
                           e.tensor_tensor(out=Pl[:, off:off + w], in0=pg[:, 0:w],
                                           in1=Ph[:, off:off + w], op=OP.subtract), s_vv, 1)
                pingF[k] = (s_vv, vv)
                off += w
            sqF[0], sqF[1], stF[0], stF[1] = pingF
            p.wait("scalar", s_vs, s_vs.n)
            sv = p.ins("scalar", lambda e, ns=len(slabs): e.activation(
                out=vtmp[:, 0:ns], in_=sume[:, 0:ns], func=AF.Copy,
                accum_out=sumt[:, :]), s_vs, 1)
            p.wait("vector", s_vs, sv)
            p.ins("vector", lambda e: e.reciprocal(out=rs[:, :], in_=sumt[:, :]), s_vv, 1)
            p_split = s_vv.n
            if DEBUG and j == DBG_J:
                p.wait("scalar", s_vv, p_split)
                p.ins("scalar", lambda e: e.dma_start(out=dbgS[:, :], in_=Ssc[:, :]),
                      s_g, 16)
                p.ins("scalar", lambda e: e.dma_start(out=dbgM[:, 0:1], in_=negmx[:, :], allow_slow_non_contiguous=True),
                      s_g, 16)
                p.ins("scalar", lambda e: e.dma_start(out=dbgM[:, 1:2], in_=sumt[:, :], allow_slow_non_contiguous=True),
                      s_g, 16)
                p.ins("scalar", lambda e: e.dma_start(out=dbgM[:, 2:3], in_=rs[:, :], allow_slow_non_contiguous=True),
                      s_g, 16)
                p.wait("scalar", s_g, s_g.n)

            p.wait("tensor", s_vv, p_split)
            p.wait2("tensor", pth_free)
            for kb in range(nkb):
                for psrc, pdst in ((Ph, PTh), (Pl, PTl)):
                    b6 = (1 if psrc is Pl else 0) ^ (kb % 2)
                    p.wait2("tensor", pt_free[b6])
                    mv = p.ins("tensor", lambda e, kb=kb, psrc=psrc, b6=b6:
                               e.transpose(pt2[b6][:, 0:128],
                                           psrc[:, kb * 128:(kb + 1) * 128], eye_b[:, :]),
                               s_mm, 1)
                    p.wait("scalar", s_mm, mv)
                    sv = p.ins("scalar", lambda e, kb=kb, pdst=pdst, b6=b6:
                               e.copy(out=pdst[:, kb * 128:(kb + 1) * 128],
                                      in_=pt2[b6][:, 0:128]), s_vs, 1)
                    pt_free[b6] = (s_vs, sv)
            pt_done = s_vs.n

            p.wait("tensor", s_vs, pt_done)
            p.wait2("tensor", state["pO_free"])
            kb = 0
            for si, (tp, rr, w) in enumerate(slabs):
                vb_ = si % 2
                nkc = w // 128
                vsh = vg[tp][2 * rr + 0][0:w, :].rearrange("(kc p) e -> p kc e", p=128)
                vsl = vg[tp][2 * rr + 1][0:w, :].rearrange("(kc p) e -> p kc e", p=128)
                vdh = Vsth[:, vb_ * 4096: vb_ * 4096 + nkc * 1024].rearrange(
                    "p (kc e) -> p kc e", e=1024)
                vdl = Vstl[:, vb_ * 4096: vb_ * 4096 + nkc * 1024].rearrange(
                    "p (kc e) -> p kc e", e=1024)
                lvv = vstS[vb_].load("sync",
                                     [lambda e, s=vsh, d=vdh: e.dma_start(out=d, in_=s),
                                      lambda e, s=vsl, d=vdl: e.dma_start(out=d, in_=s)])
                vstS[vb_].wait_loaded("tensor", lvv)
                for kc in range(nkc):
                    pth_l = PTh[:, kb * 128:(kb + 1) * 128]
                    ptl_l = PTl[:, kb * 128:(kb + 1) * 128]
                    first_g = (kb == 0)
                    last_g = (kb == nkb - 1)
                    slab_final = (kc == nkc - 1)
                    for lhs_hi, passes in ((True, ("h0", "h1", "l0", "l1")),
                                           (False, ("h0", "h1"))):
                        lhs = pth_l if lhs_hi else ptl_l
                        for ps in passes:
                            hl_v = 0 if ps[0] == "h" else 1
                            eh = int(ps[1])
                            vsrc2 = Vsth if hl_v == 0 else Vstl
                            rhs = vsrc2[:, vb_ * 4096 + kc * 1024 + eh * 512:
                                        vb_ * 4096 + kc * 1024 + (eh + 1) * 512]
                            is_first = first_g and lhs_hi and ps in ("h0", "h1")
                            is_last = last_g and (not lhs_hi) and ps in ("h0", "h1")
                            fin = slab_final and (not lhs_hi) and ps == "h1"
                            p.ins("tensor",
                                  lambda e, lhs=lhs, rhs=rhs, eh=eh, fg=is_first,
                                  lg=is_last:
                                  e.matmul(pO[:, eh * 512:(eh + 1) * 512], lhs, rhs,
                                           start=fg, stop=lg),
                                  s_mm if fin else None, 1)
                    kb += 1
                vstS[vb_].set_free((s_mm, s_mm.n))
            av_mm = s_mm.n
            pth_free = (s_mm, av_mm)
            qtS[jb].set_free((s_mm, av_mm))

            p.wait("vector", s_mm, av_mm)
            p.wait("vector", s_ost, s_ost.n)
            p.wait("vector", s_vv, s_vv.n)
            p.ins("vector", lambda e: e.tensor_scalar(out=O32[:, :], in0=pO[:, :],
                                                      scalar1=rs[:, 0:1], scalar2=None,
                                                      op0=OP.mult), s_vv, 1)
            onorm = s_vv.n
            state["pO_free"] = (s_vv, onorm)
            dbg_o_gate = 0
            if DEBUG and j == DBG_J:
                p.wait("scalar", s_vv, onorm)
                p.ins("scalar", lambda e: e.dma_start(out=dbgO[:, :], in_=O32[:, :]),
                      s_g, 16)
                dbg_o_gate = s_g.n
                p.wait("scalar", s_g, s_g.n)
            rsrc = r_dram.ap().rearrange("(dc p) q -> p dc q", p=128)[:, :,
                                                                      128 * j:128 * (j + 1)]
            rdst = Rq.ap().rearrange("p (dc q) -> p dc q", q=128)
            ldr = rq_slot.load("sync", [lambda e, s=rsrc, d=rdst: e.dma_start(out=d, in_=s)])
            p.wait("tensor", s_vv, onorm)
            if DEBUG and j == DBG_J:
                p.wait("tensor", s_g, dbg_o_gate)
            for dc in range(DC):
                bank = oc_rot[0] % 4
                oc_rot[0] += 1
                p.wait2("tensor", bank_free[bank])
                mv = p.ins("tensor", lambda e, dc=dc, bank=bank:
                           e.transpose(pb4[bank][:, 0:128], O32[:, dc * 128:(dc + 1) * 128],
                                       eye_f[:, :]), s_mm, 1)
                p.wait("vector", s_mm, mv)
                if dc == 0:
                    rq_slot.wait_loaded("vector", ldr)
                vv = p.ins("vector", lambda e, dc=dc, bank=bank:
                           e.tensor_tensor(out=O32[:, dc * 128:(dc + 1) * 128],
                                           in0=pb4[bank][:, 0:128],
                                           in1=Rq[:, dc * 128:(dc + 1) * 128], op=OP.add),
                           s_vv, 1)
                bank_free[bank] = (s_vv, vv)
            r2v = s_vv.n
            rq_slot.set_free((s_vv, r2v))
            p.wait("scalar", s_vv, r2v)
            if DEBUG and j == 0:
                p.wait("scalar", s_g, dbg_gate)
            osrc = O32.ap().rearrange("p (dc q) -> p dc q", q=128)
            odst = r_dram.ap().rearrange("(dc p) q -> p dc q", p=128)[:, :,
                                                                      128 * j:128 * (j + 1)]
            p.ins("scalar", lambda e, s=osrc, d=odst: e.dma_start(out=d, in_=s), s_ost, 16)
        r2_stores = s_ost.n

    if DEBUG and PHASES >= 3:
        dbg2 = nc.dram_tensor("dbg2", [D, TOK], F32, kind="ExternalOutput")
        p.wait("gpsimd", s_ost, r2_stores)
        p.ins("gpsimd", lambda e: e.dma_start(out=dbg2[:, :], in_=r_dram[:, :]), s_g, 16)
        p.wait("gpsimd", s_g, s_g.n)

    with nc.sbuf_tensor("ha_h2", [128, HC * TT], BF16) as ha_h, \
         nc.sbuf_tensor("ha_l2", [128, HC * TT], BF16) as ha_l, \
         nc.sbuf_tensor("hb_h2", [128, HC * TT], BF16) as hb_h, \
         nc.sbuf_tensor("hb_l2", [128, HC * TT], BF16) as hb_l:
        ctx["r_guard"] = (s_vv, s_vv.n)
        ctx["rdram_ready"] = (s_ost, r2_stores)
        bufs2 = (ha_h, ha_l, hb_h, hb_l)
        for t in range(NT if PHASES >= 4 else 0):
            done = ffn_tile(t, r_dram, y_out, False, False, bufs2)
            ctx["r_guard"] = done

    p.wait("scalar", s_w3st, s_w3st.n)
    p.wait("scalar", s_ost, s_ost.n)

    with st:
        with nc.Block() as blk:
            @blk.tensor
            def _(e):
                for f in p.ops["tensor"]:
                    f(e)

            @blk.vector
            def _(e):
                for f in p.ops["vector"]:
                    f(e)

            @blk.scalar
            def _(e):
                for f in p.ops["scalar"]:
                    f(e)

            @blk.sync
            def _(e):
                for f in p.ops["sync"]:
                    f(e)

            @blk.gpsimd
            def _(e):
                for f in p.ops["gpsimd"]:
                    f(e)
    return nc


def _shard_inputs(inputs):
    x = np.asarray(inputs["L0_attn_x"])
    import ml_dtypes
    tri = np.where(np.tril(np.ones((128, 128), dtype=bool)), 0.0, NEG).astype(np.float32)
    eye_f32 = np.eye(128, dtype=np.float32)
    eye_bf = eye_f32.astype(ml_dtypes.bfloat16)
    ones_f32 = np.ones((128, 128), np.float32)
    zeros = np.zeros((128, 128), np.float32)
    full = np.full((128, 128), NEG, np.float32)
    common = {
        "w1": np.ascontiguousarray(np.asarray(inputs["L0_ffn_w1_w"], np.float32).T),
        "w2": np.ascontiguousarray(np.asarray(inputs["L0_ffn_w2_w"], np.float32).T),
        "h1": np.ascontiguousarray(np.asarray(inputs["L0_ffn_h1_w"], np.float32).T),
        "h2": np.ascontiguousarray(np.asarray(inputs["L0_ffn_h2_w"], np.float32).T),
        "w3": np.ascontiguousarray(np.asarray(inputs["L0_ffn_w3_w"], np.float32).T),
        "wq": np.ascontiguousarray(inputs["L0_attn_Wq"], np.float32),
        "wk": np.ascontiguousarray(inputs["L0_attn_Wk"], np.float32),
        "wv": np.ascontiguousarray(inputs["L0_attn_Wv"], np.float32),
        "s1": np.ascontiguousarray(np.asarray(inputs["L0_ffn_w1_s"], np.float32).reshape(-1, 128).T),
        "s2": np.ascontiguousarray(np.asarray(inputs["L0_ffn_w2_s"], np.float32).reshape(-1, 128).T),
        "s3": np.ascontiguousarray(np.asarray(inputs["L0_ffn_h1_s"], np.float32).reshape(-1, 128).T),
        "s4": np.ascontiguousarray(np.asarray(inputs["L0_ffn_h2_s"], np.float32).reshape(-1, 128).T),
        "s5": np.ascontiguousarray(np.asarray(inputs["L0_ffn_w3_s"], np.float32).reshape(-1, 128).T),
    }
    maps = []
    for c in range(8):
        b, par = divmod(c, 2)
        loc = x[b].reshape(32, 128, D)[par::2].reshape(TOK, D)
        m = dict(common)
        m["x"] = np.ascontiguousarray(loc.T)
        m["mask0"] = tri if par == 0 else zeros
        m["mask1"] = full if par == 0 else tri
        m["eyeb"] = eye_bf
        m["eyef"] = eye_f32
        m["ones"] = ones_f32
        maps.append(m)
    return maps


def kernel(**inputs):
    maps = _shard_inputs(inputs)
    nc = build_program()
    res = run_bass_kernel_spmd(nc, maps, core_ids=list(range(8)))
    out = np.empty((4, 4096, D), np.float32)
    for c in range(8):
        b, par = divmod(c, 2)
        y = np.asarray(res.results[c]["y"])
        out[b].reshape(32, 128, D)[par::2] = y.T.reshape(16, 128, D)
    return out
